# version 08
# baseline (speedup 1.0000x reference)
import numpy as np
import contextlib
import concourse.bass as bass
import concourse.mybir as mybir
from concourse.bass_utils import run_bass_kernel_spmd

F32 = mybir.dt.float32
BF16 = mybir.dt.bfloat16
AF = mybir.ActivationFunctionType
ALU = mybir.AluOpType
AX = mybir.AxisListType

B, T, C = 4, 8192, 1024
NH, HD = 16, 64
E, HFF = 8, 4096
NCORE = 8
TLOC = 4096
CCH = 8
NPAIR = 8
NTT = 16
JC = 64
NIB = 4
HC2 = 16
NSLAB = 4

ENGS = ("PE", "ACT", "VEC", "GP", "SY")
PHASES = {"p0", "p1", "p2", "p3a", "p3b"}
P0STAGE = 99
P1PAIRS = 99
P1STAGE = 99
P1IBS = 99
P1JCS = 99
P2TTS = 99
P2STAGE = 99
P3EHS = 99
P3STAGE = 99
P3BSTAGE = 99


class Sem:
    def __init__(self, handle):
        self.h = handle
        self.c = 0


class Ctx:
    def __init__(self, nc, key, eng, sems):
        self.nc = nc
        self.key = key
        self.eng = eng
        self.s = sems

    def inc(self, ins, sem, n=1):
        sem.c += n
        if ins is not None:
            ins.then_inc(sem.h, n)
        return ins

    def wait(self, key, sem, val=None):
        if self.key == key:
            self.eng.wait_ge(sem.h, sem.c if val is None else val)

    def barrier(self, *semnames):
        for key in ("PE", "ACT", "VEC"):
            for sn in semnames:
                if sn in ("pe",) and key == "PE":
                    continue
                if sn in ("act",) and key == "ACT":
                    continue
                if sn in ("vec",) and key == "VEC":
                    continue
                self.wait(key, self.s[sn])

    def mm(self, out, lhsT, rhs, start=True, stop=True, tp=None, sgc=False):
        if self.key == "PE":
            return self.nc.tensor.matmul(out, lhsT, rhs, start=start, stop=stop,
                                         tile_position=tp, skip_group_check=sgc)
        return None

    def tr(self, out, in_, ident):
        if self.key == "PE":
            return self.nc.tensor.transpose(out, in_, ident)
        return None

    def act(self, out, in_, fn, bias=0.0, scale=1.0):
        if self.key == "ACT":
            return self.nc.scalar.activation(out, in_, fn, bias=bias, scale=scale)
        return None

    def ts(self, out, in0, s1, s2, op0, op1=None):
        if self.key == "VEC":
            if op1 is None:
                return self.nc.vector.tensor_scalar(out, in0, s1, None, op0)
            return self.nc.vector.tensor_scalar(out, in0, s1, s2, op0, op1)
        return None

    def tt(self, out, in0, in1, op):
        if self.key == "VEC":
            return self.nc.vector.tensor_tensor(out, in0, in1, op)
        return None

    def stt(self, out, in0, s, in1, op0, op1):
        if self.key == "VEC":
            return self.nc.vector.scalar_tensor_tensor(out, in0, s, in1, op0, op1)
        return None

    def vcopy(self, out, in_):
        if self.key == "VEC":
            return self.nc.vector.tensor_copy(out, in_)
        return None

    def vred(self, out, in_, op):
        if self.key == "VEC":
            return self.nc.vector.tensor_reduce(out, in_, AX.X, op)
        return None

    def vrecip(self, out, in_):
        if self.key == "VEC":
            return self.nc.vector.reciprocal(out, in_)
        return None

    def dma(self, q, out, in_):
        if self.key == q:
            if q == "GP":
                return self.nc.gpsimd.dma_start(out, in_)
            return self.nc.sync.dma_start(out, in_)
        return None

    def memset(self, ap, v):
        if self.key == "GP":
            return self.nc.gpsimd.memset(ap, v)
        return None

    def gnop(self):
        if self.key == "GP":
            return self.nc.gpsimd.nop()
        return None


def run_phase(nc, blk, sems, fn):
    snap = {k: s.c for k, s in sems.items()}
    finals = []
    engmap = {"PE": blk.tensor, "ACT": blk.scalar, "VEC": blk.vector,
              "GP": blk.gpsimd, "SY": blk.sync}
    for key in ENGS:
        for k, s in sems.items():
            s.c = snap[k]
        def body(eng, key=key):
            fn(Ctx(nc, key, eng, sems))
        engmap[key](body)
        finals.append({k: s.c for k, s in sems.items()})
    for f in finals[1:]:
        assert f == finals[0], f"engine walk divergence: {finals[0]} vs {f}"


DETECT_RACES = True


def build_program():
    nc = bass.Bass("TRN2", target_bir_lowering=False,
                   detect_race_conditions=DETECT_RACES)

    xT = nc.dram_tensor("xT", [C, T], F32, kind="ExternalInput").ap()
    wqkv = nc.dram_tensor("wqkv", [C, 3 * C], F32, kind="ExternalInput").ap()
    bqkvT = nc.dram_tensor("bqkvT", [128, 24], F32, kind="ExternalInput").ap()
    sigma = nc.dram_tensor("sigma", [1, 1], F32, kind="ExternalInput").ap()
    wproj = nc.dram_tensor("wproj", [C, C], F32, kind="ExternalInput").ap()
    bprojT = nc.dram_tensor("bprojT", [128, 8], F32, kind="ExternalInput").ap()
    wrT = nc.dram_tensor("wrT", [C, E], F32, kind="ExternalInput").ap()
    w1 = nc.dram_tensor("w1", [E, C, HFF], F32, kind="ExternalInput").ap()
    b1T = nc.dram_tensor("b1T", [128, 256], F32, kind="ExternalInput").ap()
    w2 = nc.dram_tensor("w2", [E, HFF, C], F32, kind="ExternalInput").ap()
    b2 = nc.dram_tensor("b2", [E, C], F32, kind="ExternalInput").ap()
    out = nc.dram_tensor("out", [TLOC, C], F32, kind="ExternalOutput").ap()
    spf = nc.dram_tensor("spf", [C, TLOC], F32, kind="Internal").ap()
    spb = nc.dram_tensor("spb", [C, TLOC], BF16, kind="Internal").ap()
    cfmh = nc.dram_tensor("cfmh", [E, TLOC], F32, kind="Internal").ap()

    marks = {}
    stack = contextlib.ExitStack()
    with stack:
        sems = {}
        for name in ("dX", "dWp", "dW1", "dW2", "dA", "dS",
                     "pe", "act", "vec", "gp"):
            sems[name] = Sem(stack.enter_context(nc.semaphore(name)))

        kc = stack.enter_context(nc.sbuf_tensor("kc", [128, 4], F32))
        ones1 = stack.enter_context(nc.sbuf_tensor("ones1", [1, 128], F32))
        onescol = stack.enter_context(nc.sbuf_tensor("onescol", [128, 1], F32))
        mask64 = stack.enter_context(nc.sbuf_tensor("mask64", [128, 2], F32))
        ind2 = stack.enter_context(nc.sbuf_tensor("ind2", [2, 128], F32))
        idb = stack.enter_context(nc.sbuf_tensor("idb", [128, 128], BF16))
        idf = stack.enter_context(nc.sbuf_tensor("idf", [128, 128], F32))
        ssg = stack.enter_context(nc.sbuf_tensor("ssg", [1, 1], F32))
        sgc = stack.enter_context(nc.sbuf_tensor("sgc", [128, 5], F32))
        sbq = stack.enter_context(nc.sbuf_tensor("sbq", [128, 24], F32))
        sbp = stack.enter_context(nc.sbuf_tensor("sbp", [128, 8], F32))
        sb1 = stack.enter_context(nc.sbuf_tensor("sb1", [128, 256], F32))
        swr = stack.enter_context(nc.sbuf_tensor("swr", [128, CCH * E], F32))
        ctm = stack.enter_context(
            nc.sbuf_tensor("ctm", [128, (TLOC // 128) * E], F32))

        with nc.psum_tensor("ps0", [128, 128], F32) as ps0, nc.Block() as blk:
            def p0(c: Ctx):
                s = c.s
                for ap, v in ((kc[:, 0:1], 1e-5), (kc[:, 1:2], -0.5),
                              (ones1[:], 1.0), (onescol[:], 1.0),
                              (mask64[:], 0.0), (idb[:], 0.0), (idf[:], 0.0)):
                    c.inc(c.memset(ap, v), s["gp"])
                c.wait("GP", s["gp"])
                c.inc(c.memset(mask64[0:64, 0:1], 1.0), s["gp"])
                c.inc(c.memset(mask64[64:128, 1:2], 1.0), s["gp"])
                for ident in (idb, idf):
                    if c.key == "GP":
                        ins = nc.gpsimd.affine_select(
                            out=ident[:], in_=ident[:], compare_op=ALU.not_equal,
                            fill=1.0, base=0, pattern=[[-1, 128]],
                            channel_multiplier=1)
                    else:
                        ins = None
                    c.inc(ins, s["gp"])
                if P0STAGE < 2:
                    return
                c.inc(c.dma("SY", ssg[:], sigma[:]), s["dA"], 16)
                c.inc(c.dma("SY", sbq[:], bqkvT[:]), s["dA"], 16)
                c.inc(c.dma("SY", sbp[:], bprojT[:]), s["dA"], 16)
                c.inc(c.dma("SY", sb1[:], b1T[:]), s["dA"], 16)
                c.inc(c.dma("SY", swr[:].rearrange("p (c e) -> p c e", e=E),
                            wrT.rearrange("(c p) e -> p c e", p=128)),
                      s["dA"], 16)
                if P0STAGE < 3:
                    return
                c.wait("PE", s["dA"])
                c.wait("PE", s["gp"])
                c.inc(c.mm(ps0[:, 0:1], ones1[:], ssg[:]), s["pe"])
                c.wait("VEC", s["pe"])
                c.inc(c.vcopy(sgc[:, 0:1], ps0[:, 0:1]), s["vec"])
                c.wait("VEC", s["vec"])
                c.inc(c.ts(sgc[:, 1:2], sgc[:, 0:1], 2.0, None, ALU.mult),
                      s["vec"])
                c.inc(c.ts(sgc[:, 2:3], sgc[:, 0:1], -1.0, None, ALU.mult),
                      s["vec"])
                c.inc(c.ts(sgc[:, 3:4], sgc[:, 0:1], -64.0, None, ALU.mult),
                      s["vec"])
                c.inc(c.ts(sgc[:, 4:5], sgc[:, 0:1], 64.0, None, ALU.mult),
                      s["vec"])
                if P0STAGE < 4:
                    return
                c.wait("PE", s["vec"])
                c.inc(c.tr(ps0[0:2, 0:128], mask64[:], idf[:]), s["pe"])
                c.wait("VEC", s["pe"])
                c.inc(c.vcopy(ind2[:], ps0[0:2, 0:128]), s["vec"])

            run_phase(nc, blk, sems, p0 if "p0" in PHASES else (lambda c: None))

        with nc.sbuf_tensor("ot", [128, NPAIR * TLOC], BF16) as ot:
            with contextlib.ExitStack() as es1:
                ec = es1.enter_context
                xst = ec(nc.sbuf_tensor("xst", [128, 2 * CCH * 512], BF16))
                wqsb = ec(nc.sbuf_tensor("wqsb", [128, CCH * 128], BF16))
                wksb = ec(nc.sbuf_tensor("wksb", [128, CCH * 128], BF16))
                wvsb = ec(nc.sbuf_tensor("wvsb", [128, CCH * 128], BF16))
                qt = ec(nc.sbuf_tensor("qt", [128, T], BF16))
                vtA = ec(nc.sbuf_tensor("vtA", [128, T], BF16))
                vtB = ec(nc.sbuf_tensor("vtB", [128, T], BF16))
                ktA = ec(nc.sbuf_tensor("ktA", [128, TLOC], BF16))
                ktB = ec(nc.sbuf_tensor("ktB", [128, TLOC], BF16))
                vbb = ec(nc.sbuf_tensor("vbb", [128, 1024], BF16))
                pt = ec(nc.sbuf_tensor("pt", [128, 2 * 2048], BF16))
                qnrow = ec(nc.sbuf_tensor("qnrow", [2, 512], F32))
                qnT = ec(nc.sbuf_tensor("qnT", [128, 128], F32))
                biasm = ec(nc.sbuf_tensor("biasm", [128, 128], F32))
                knfm = ec(nc.sbuf_tensor("knfm", [2, TLOC], F32))
                rhat = ec(nc.sbuf_tensor("rhat", [128, 1024], F32))
                sqv = ec(nc.sbuf_tensor("sqv", [128, 4 * 512], F32))
                psS = ec(nc.psum_tensor("psS", [128, 2048], F32))
                psO = ec(nc.psum_tensor("psO", [128, 1024], F32))
                psT = ec(nc.psum_tensor("psT", [128, 1024], BF16))
                psX = ec(nc.psum_tensor("psX", [128, 512], F32))
                blk = ec(nc.Block())
                def p1(c: Ctx):
                    s = c.s
                    c.inc(c.memset(ktA[64:128, :], 0.0), s["gp"])
                    c.inc(c.memset(ktB[0:64, :], 0.0), s["gp"])
                    c.inc(c.memset(vtA[:], 0.0), s["gp"])
                    c.inc(c.memset(vtB[:], 0.0), s["gp"])
                    c.wait("PE", s["gp"])
                    qkv_prev = 0
                    for p in range(min(NPAIR, P1PAIRS)):
                        c.barrier("pe", "act", "vec", "gp")
                        c.wait("GP", s["pe"], qkv_prev)
                        for wsb, off in ((wqsb, 0), (wksb, C), (wvsb, 2 * C)):
                            wd = wsb[:].rearrange("p (c m) -> p c m", c=CCH)
                            ws = (wqkv[:, off + p * 128:off + (p + 1) * 128]
                                  .rearrange("(c p) m -> p c m", p=128))
                            c.inc(c.dma("GP", wd[:, 0:4, :], ws[:, 0:4, :]),
                                  s["dWp"], 16)
                            c.inc(c.dma("GP", wd[:, 4:8, :], ws[:, 4:8, :]),
                                  s["dWp"], 16)
                        mm_done = {}
                        ev_sq = {}

                        def qkv_banks(tt):
                            if tt % 2 == 0:
                                return (psS[:, 0:512], psS[:, 512:1024],
                                        psS[:, 1024:1536])
                            return (psS[:, 1536:2048], psO[:, 0:512],
                                    psO[:, 512:1024])

                        def emit_tail(tt):
                            local = tt < 8
                            bq, bv, bk = qkv_banks(tt)
                            c.wait("PE", s["act"], ev_sq[tt][1])
                            c.wait("PE", s["vec"])
                            c.inc(c.mm(psX[0:2, 0:512], mask64[:],
                                       sqv[:, 0:512] if tt % 2 == 0 else
                                       sqv[:, 1024:1536]), s["pe"])
                            c.wait("VEC", s["pe"])
                            c.inc(c.vcopy(qnrow[:], psX[0:2, 0:512]), s["vec"])
                            qnrow_done = s["vec"].c
                            if local:
                                c.wait("PE", s["vec"], qnrow_done)
                                c.inc(c.mm(psX[0:2, 0:512], mask64[:],
                                           sqv[:, 512:1024] if tt % 2 == 0
                                           else sqv[:, 1536:2048]), s["pe"])
                                c.wait("VEC", s["pe"])
                                c.inc(c.vcopy(knfm[0:2, tt * 512:
                                              (tt + 1) * 512],
                                              psX[0:2, 0:512]), s["vec"])
                            c.wait("PE", s["vec"])
                            for k in range(4):
                                ins = c.tr(psT[:, k * 128:(k + 1) * 128],
                                           vbb[:, (tt % 2) * 512 + k * 128:
                                               (tt % 2) * 512 + (k + 1) * 128],
                                           idb[:])
                            c.inc(ins, s["pe"])
                            c.wait("VEC", s["pe"])
                            for k in range(4):
                                jcg = tt * 4 + k
                                c.inc(c.vcopy(
                                    vtA[:, jcg * 128:jcg * 128 + 64],
                                    psT[:, k * 128:k * 128 + 64]), s["vec"])
                                c.inc(c.vcopy(
                                    vtB[:, jcg * 128 + 64:jcg * 128 + 128],
                                    psT[:, k * 128 + 64:k * 128 + 128]),
                                    s["vec"])
                            c.wait("PE", s["vec"], qnrow_done)
                            for k in range(4):
                                ins = c.tr(psX[:, 256 + k * 2:258 + k * 2],
                                           qnrow[:, k * 128:(k + 1) * 128],
                                           idf[0:2, 0:2])
                            c.inc(ins, s["pe"])
                            c.wait("VEC", s["pe"])
                            c.inc(c.vcopy(qnT[:, tt * 8:tt * 8 + 8],
                                          psX[:, 256:264]), s["vec"])

                        for tt in range(NTT):
                            buf = tt % 2
                            xv = xst[:, buf * CCH * 512:(buf + 1) * CCH * 512]
                            if tt >= 2:
                                c.wait("GP", s["pe"], mm_done[tt - 2])
                            else:
                                c.wait("GP", s["pe"], qkv_prev)
                            xsrc = (xT[:, tt * 512:(tt + 1) * 512]
                                    .rearrange("(c p) t -> p c t", p=128))
                            xdst = xv.rearrange("p (c t) -> p c t", c=CCH)
                            c.inc(c.dma("GP", xdst[:, 0:4, :], xsrc[:, 0:4, :]),
                                  s["dX"], 16)
                            c.inc(c.dma("GP", xdst[:, 4:8, :], xsrc[:, 4:8, :]),
                                  s["dX"], 16)
                            local = tt < 8
                            bq, bv, bk = qkv_banks(tt)
                            c.wait("PE", s["dX"])
                            if tt == 0:
                                c.wait("PE", s["dWp"])
                            if tt >= 2:
                                c.wait("PE", s["vec"], ev_sq[tt - 2][0])
                                c.wait("PE", s["act"], ev_sq[tt - 2][1])
                            for cc in range(CCH):
                                ins = c.mm(bq, wqsb[:, cc * 128:(cc + 1) * 128],
                                           xv[:, cc * 512:(cc + 1) * 512],
                                           start=(cc == 0), stop=(cc == CCH - 1))
                            c.inc(ins, s["pe"])
                            for cc in range(CCH):
                                ins = c.mm(bv, wvsb[:, cc * 128:(cc + 1) * 128],
                                           xv[:, cc * 512:(cc + 1) * 512],
                                           start=(cc == 0), stop=(cc == CCH - 1))
                            c.inc(ins, s["pe"])
                            if local:
                                for cc in range(CCH):
                                    ins = c.mm(bk,
                                               wksb[:, cc * 128:(cc + 1) * 128],
                                               xv[:, cc * 512:(cc + 1) * 512],
                                               start=(cc == 0),
                                               stop=(cc == CCH - 1))
                                c.inc(ins, s["pe"])
                            mm_done[tt] = s["pe"].c
                            sq0 = 0 if tt % 2 == 0 else 1024
                            c.wait("ACT", s["pe"], mm_done[tt])
                            c.inc(c.act(sqv[:, sq0:sq0 + 512], bq, AF.Square,
                                        bias=sbq[:, p:p + 1]), s["act"])
                            if local:
                                c.inc(c.act(sqv[:, sq0 + 512:sq0 + 1024], bk,
                                            AF.Square,
                                            bias=sbq[:, 8 + p:9 + p]), s["act"])
                            sq_done = s["act"].c
                            c.wait("VEC", s["pe"], mm_done[tt])
                            c.wait("VEC", s["act"], sq_done)
                            c.inc(c.ts(qt[:, tt * 512:(tt + 1) * 512], bq,
                                       sbq[:, p:p + 1], None, ALU.add),
                                  s["vec"])
                            c.inc(c.ts(vbb[:, (tt % 2) * 512:
                                           (tt % 2) * 512 + 512], bv,
                                       sbq[:, 16 + p:17 + p], None, ALU.add),
                                  s["vec"])
                            if local:
                                c.inc(c.ts(ktA[0:64, tt * 512:(tt + 1) * 512],
                                           bk[0:64, :] if tt % 2 == 0 else
                                           psO[0:64, 512:1024],
                                           sbq[0:64, 8 + p:9 + p], None,
                                           ALU.add), s["vec"])
                                c.inc(c.ts(ktB[64:128, tt * 512:(tt + 1) * 512],
                                           bk[64:128, :] if tt % 2 == 0 else
                                           psO[64:128, 512:1024],
                                           sbq[64:128, 8 + p:9 + p], None,
                                           ALU.add), s["vec"])
                            ev_sq[tt] = (s["vec"].c, s["act"].c)
                            if tt >= 1:
                                emit_tail(tt - 1)
                        emit_tail(NTT - 1)
                        qkv_prev = s["pe"].c
                        if P1STAGE < 2:
                            continue
                        c.wait("VEC", s["vec"])
                        c.inc(c.ts(biasm[:], qnT[:], sgc[:, 2:3], sgc[:, 3:4],
                                   ALU.mult, ALU.add), s["vec"])
                        c.wait("ACT", s["vec"])
                        c.wait("ACT", s["pe"])
                        c.inc(c.act(knfm[:], knfm[:], AF.Exp,
                                    bias=sgc[0:2, 4:5], scale=sgc[0:2, 2:3]),
                              s["act"])
                        stats_act = s["act"].c
                        if P1STAGE < 3:
                            continue
                        for ib in range(min(NIB, P1IBS)):
                            c.wait("PE", s["act"], stats_act)
                            c.wait("PE", s["vec"])
                            for it in range(2):
                                if it == 1:
                                    c.wait("PE", s["vec"])
                                c.inc(c.mm(psX[:, 0:512], ind2[:],
                                           knfm[0:2,
                                                ib * 1024 + it * 512:
                                                ib * 1024 + (it + 1) * 512]),
                                      s["pe"])
                                c.wait("VEC", s["pe"])
                                c.inc(c.vcopy(rhat[:, it * 512:(it + 1) * 512],
                                              psX[:, 0:512]), s["vec"])
                            expdone = {}
                            odone = {}
                            sdone = {}

                            def emit_S(jc, head):
                                base = 0 if head == 0 else 1024
                                ktp = ktA if head == 0 else ktB
                                for it in range(2):
                                    ins = c.mm(
                                        psS[:, base + it * 512:
                                            base + (it + 1) * 512],
                                        qt[:, jc * 128:(jc + 1) * 128],
                                        ktp[:, ib * 1024 + it * 512:
                                            ib * 1024 + (it + 1) * 512])
                                c.inc(ins, s["pe"])
                                return s["pe"].c

                            def emit_O(jc, head):
                                pb = (jc % 2) * 2048 + head * 1024
                                vtp = vtA if head == 0 else vtB
                                for it in range(2):
                                    ins = c.mm(
                                        psO[:, it * 512:(it + 1) * 512],
                                        vtp[:, jc * 128:(jc + 1) * 128],
                                        pt[:, pb + it * 512:pb + (it + 1) * 512],
                                        start=(jc == 0 and head == 0),
                                        stop=(jc == min(JC, P1JCS) - 1
                                              and head == 1),
                                        sgc=True)
                                c.inc(ins, s["pe"])
                                return s["pe"].c

                            def emit_exp(jc, head):
                                pb = (jc % 2) * 2048 + head * 1024
                                base = 0 if head == 0 else 1024
                                ins = c.act(pt[:, pb:pb + 1024],
                                            psS[:, base:base + 1024], AF.Exp,
                                            bias=biasm[:, jc * 2 + head:
                                                       jc * 2 + head + 1],
                                            scale=sgc[:, 1:2])
                                c.inc(ins, s["act"])
                                return s["act"].c

                            sA = emit_S(0, 0)
                            sB = emit_S(0, 1)
                            sdone[0] = (sA, sB)
                            NJC = min(JC, P1JCS)
                            for jc in range(NJC):
                                c.wait("ACT", s["pe"], sdone[jc][0])
                                if jc >= 2:
                                    c.wait("ACT", s["pe"], odone[jc - 2][0])
                                eA = emit_exp(jc, 0)
                                c.wait("ACT", s["pe"], sdone[jc][1])
                                if jc >= 2:
                                    c.wait("ACT", s["pe"], odone[jc - 2][1])
                                eB = emit_exp(jc, 1)
                                expdone[jc] = (eA, eB)
                                c.wait("PE", s["act"], eA)
                                oA = emit_O(jc, 0)
                                if jc + 1 < NJC:
                                    nsA = emit_S(jc + 1, 0)
                                c.wait("PE", s["act"], eB)
                                oB = emit_O(jc, 1)
                                if jc + 1 < NJC:
                                    nsB = emit_S(jc + 1, 1)
                                    sdone[jc + 1] = (nsA, nsB)
                                odone[jc] = (oA, oB)
                            c.wait("VEC", s["pe"])
                            for it in range(2):
                                c.inc(c.tt(
                                    ot[:, p * TLOC + ib * 1024 + it * 512:
                                       p * TLOC + ib * 1024 + (it + 1) * 512],
                                    psO[:, it * 512:(it + 1) * 512],
                                    rhat[:, it * 512:(it + 1) * 512],
                                    ALU.mult), s["vec"])
                            c.wait("PE", s["vec"])

                run_phase(nc, blk, sems, p1 if "p1" in PHASES else (lambda c: None))

            with contextlib.ExitStack() as es2:
                ec = es2.enter_context
                wpsb = ec(nc.sbuf_tensor("wpsb", [128, CCH * C], BF16))
                xls = ec(nc.sbuf_tensor("xls", [128, 2 * CCH * 512], F32))
                x1t = ec(nc.sbuf_tensor("x1t", [128, CCH * 512], F32))
                x1bf = ec(nc.sbuf_tensor("x1bf", [128, CCH * 512], BF16))
                msb = ec(nc.sbuf_tensor("msb", [128, 512], F32))
                rsb = ec(nc.sbuf_tensor("rsb", [128, 512], F32))
                mrow = ec(nc.sbuf_tensor("mrow", [1, 512], F32))
                vrow = ec(nc.sbuf_tensor("vrow", [1, 512], F32))
                sq2 = ec(nc.sbuf_tensor("sq2", [128, 2 * 512], F32))
                smx = ec(nc.sbuf_tensor("smx", [128, 4], F32))
                prb = ec(nc.sbuf_tensor("prb", [128, 16], F32))
                cfm = ec(nc.sbuf_tensor("cfm", [8, TLOC], F32))
                psP = ec(nc.psum_tensor("psP", [128, 1024], F32))
                psM = ec(nc.psum_tensor("psM", [128, 512], F32))
                psR2 = ec(nc.psum_tensor("psR2", [128, 512], F32))
                psSt = ec(nc.psum_tensor("psSt", [128, 512], F32))
                psRt = ec(nc.psum_tensor("psRt", [128, 8], F32))
                blk = ec(nc.Block())
                def p2(c: Ctx):
                    s = c.s
                    if P2STAGE != -1:
                        c.barrier("pe", "act", "vec")
                    if P2STAGE != -2:
                        c.wait("GP", s["pe"])
                        wpd = wpsb[:].rearrange("p (c m) -> p c m", c=CCH)
                        wps = wproj[:].rearrange("(c p) m -> p c m", p=128)
                        c.inc(c.dma("GP", wpd[:, 0:4, :], wps[:, 0:4, :]),
                              s["dWp"], 16)
                        c.inc(c.dma("GP", wpd[:, 4:8, :], wps[:, 4:8, :]),
                              s["dWp"], 16)
                    evco = {}
                    for tt in range(min(CCH, P2TTS)):
                        c.barrier("pe", "act", "vec")
                        buf = tt % 2
                        xv = xls[:, buf * CCH * 512:(buf + 1) * CCH * 512]
                        c.wait("SY", s["vec"])
                        xsrc2 = (xT[:, tt * 512:(tt + 1) * 512]
                                 .rearrange("(c p) t -> p c t", p=128))
                        xdst2 = xv.rearrange("p (c t) -> p c t", c=CCH)
                        c.inc(c.dma("SY", xdst2[:, 0:4, :], xsrc2[:, 0:4, :]),
                              s["dA"], 16)
                        c.inc(c.dma("SY", xdst2[:, 4:8, :], xsrc2[:, 4:8, :]),
                              s["dA"], 16)
                        c.wait("VEC", s["dS"])
                        if tt == 0:
                            c.wait("PE", s["dWp"])
                        if P2STAGE < 1:
                            continue
                        for co in range(CCH):
                            pp = psP[:, (co % 2) * 512:(co % 2) * 512 + 512]
                            if co >= 2:
                                c.wait("PE", s["vec"], evco[co - 2])
                            for ci in range(CCH):
                                ins = c.mm(pp,
                                           wpsb[:, ci * C + co * 128:
                                                ci * C + (co + 1) * 128],
                                           ot[:, ci * TLOC + tt * 512:
                                              ci * TLOC + (tt + 1) * 512],
                                           start=(ci == 0), stop=(ci == CCH - 1))
                            c.inc(ins, s["pe"])
                            c.wait("VEC", s["pe"])
                            c.wait("VEC", s["dA"])
                            c.inc(c.stt(x1t[:, co * 512:(co + 1) * 512], pp,
                                        sbp[:, co:co + 1],
                                        xv[:, co * 512:(co + 1) * 512],
                                        ALU.add, ALU.add), s["vec"])
                            evco[co] = s["vec"].c
                        x1_done = s["vec"].c
                        if P2STAGE < 2:
                            continue
                        c.wait("PE", s["vec"], x1_done)
                        for ci in range(CCH):
                            ins = c.mm(psSt[0:1, 0:512], onescol[:],
                                       x1t[:, ci * 512:(ci + 1) * 512],
                                       start=(ci == 0), stop=(ci == CCH - 1))
                        c.inc(ins, s["pe"])
                        c.wait("VEC", s["pe"])
                        c.inc(c.ts(mrow[:], psSt[0:1, 0:512], 1.0 / C, None,
                                   ALU.mult), s["vec"])
                        c.wait("PE", s["vec"])
                        c.inc(c.mm(psM[:], ones1[:], mrow[:]), s["pe"])
                        c.wait("VEC", s["pe"])
                        c.inc(c.vcopy(msb[:], psM[:]), s["vec"])
                        c.wait("VEC", s["vec"])
                        for ci in range(CCH):
                            c.inc(c.tt(xv[:, ci * 512:(ci + 1) * 512],
                                       x1t[:, ci * 512:(ci + 1) * 512],
                                       msb[:], ALU.subtract), s["vec"])
                        d_done = s["vec"].c
                        if P2STAGE < 3:
                            continue
                        sqmm = {}
                        for ci in range(CCH):
                            c.wait("ACT", s["vec"], d_done)
                            if ci >= 2:
                                c.wait("ACT", s["pe"], sqmm[ci - 2])
                            c.inc(c.act(sq2[:, (ci % 2) * 512:
                                            (ci % 2) * 512 + 512],
                                        xv[:, ci * 512:(ci + 1) * 512],
                                        AF.Square), s["act"])
                            c.wait("PE", s["act"])
                            c.inc(c.mm(psSt[0:1, 0:512], onescol[:],
                                       sq2[:, (ci % 2) * 512:
                                           (ci % 2) * 512 + 512],
                                       start=(ci == 0), stop=(ci == CCH - 1)),
                                  s["pe"])
                            sqmm[ci] = s["pe"].c
                        c.wait("VEC", s["pe"])
                        c.inc(c.ts(vrow[:], psSt[0:1, 0:512], 1.0 / C, None,
                                   ALU.mult), s["vec"])
                        c.wait("ACT", s["vec"])
                        c.inc(c.act(vrow[:], vrow[:], AF.Ln, bias=kc[0:1, 0:1]),
                              s["act"])
                        c.wait("ACT", s["act"])
                        c.inc(c.act(vrow[:], vrow[:], AF.Exp,
                                    scale=kc[0:1, 1:2]), s["act"])
                        c.wait("PE", s["act"])
                        c.inc(c.mm(psR2[:], ones1[:], vrow[:]), s["pe"])
                        c.wait("VEC", s["pe"])
                        c.inc(c.vcopy(rsb[:], psR2[:]), s["vec"])
                        c.wait("VEC", s["vec"])
                        for ci in range(CCH):
                            c.inc(c.tt(x1t[:, ci * 512:(ci + 1) * 512],
                                       xv[:, ci * 512:(ci + 1) * 512],
                                       rsb[:], ALU.mult), s["vec"])
                        norm_done = s["vec"].c
                        c.wait("VEC", s["vec"], norm_done)
                        for ci in range(CCH):
                            c.inc(c.vcopy(x1bf[:, ci * 512:(ci + 1) * 512],
                                          x1t[:, ci * 512:(ci + 1) * 512]),
                                  s["vec"])
                        bf_done = s["vec"].c
                        if P2STAGE < 4:
                            continue
                        c.wait("SY", s["vec"], bf_done)
                        spfd = (spf[:, tt * 512:(tt + 1) * 512]
                                .rearrange("(c p) t -> p c t", p=128))
                        x1s = x1t[:].rearrange("p (c t) -> p c t", c=CCH)
                        spbd = (spb[:, tt * 512:(tt + 1) * 512]
                                .rearrange("(c p) t -> p c t", p=128))
                        x1bs = x1bf[:].rearrange("p (c t) -> p c t", c=CCH)
                        for lo, hi in ((0, 4), (4, 8)):
                            c.inc(c.dma("SY", spfd[:, lo:hi, :],
                                        x1s[:, lo:hi, :]), s["dS"], 16)
                            c.inc(c.dma("SY", spbd[:, lo:hi, :],
                                        x1bs[:, lo:hi, :]), s["dS"], 16)
                        if P2STAGE < 5:
                            continue
                        for t4 in range(4):
                            c.wait("PE", s["vec"], norm_done)
                            c.wait("PE", s["vec"])
                            c.wait("PE", s["act"])
                            for ci in range(CCH):
                                ins = c.mm(psRt[:],
                                           x1t[:, ci * 512 + t4 * 128:
                                               ci * 512 + (t4 + 1) * 128],
                                           swr[:, ci * E:(ci + 1) * E],
                                           start=(ci == 0), stop=(ci == CCH - 1))
                            c.inc(ins, s["pe"])

                            def v1(ins):
                                c.inc(ins, s["vec"])
                                c.wait("VEC", s["vec"])
                            c.wait("VEC", s["pe"])
                            v1(c.vred(smx[:, 0:1], psRt[:], ALU.max))
                            v1(c.ts(smx[:, 1:2], smx[:, 0:1], -1.0, None,
                                    ALU.mult))
                            c.wait("ACT", s["vec"])
                            c.wait("ACT", s["pe"])
                            c.inc(c.act(prb[:, 0:8], psRt[:], AF.Exp,
                                        bias=smx[:, 1:2]), s["act"])
                            c.wait("VEC", s["act"])
                            v1(c.vred(smx[:, 2:3], prb[:, 0:8], ALU.add))
                            v1(c.vrecip(smx[:, 3:4], smx[:, 2:3]))
                            v1(c.ts(prb[:, 0:8], prb[:, 0:8], smx[:, 3:4],
                                    None, ALU.mult))
                            v1(c.vred(smx[:, 0:1], prb[:, 0:8], ALU.max))
                            v1(c.ts(prb[:, 8:16], prb[:, 0:8], smx[:, 0:1],
                                    None, ALU.is_ge))
                            v1(c.stt(prb[:, 8:16], prb[:, 8:16], -1e30,
                                     prb[:, 0:8], ALU.mult, ALU.add))
                            v1(c.vred(smx[:, 1:2], prb[:, 8:16], ALU.max))
                            v1(c.ts(prb[:, 8:16], prb[:, 0:8], smx[:, 1:2],
                                    None, ALU.is_ge))
                            c.inc(c.tt(ctm[:, (tt * 4 + t4) * E:
                                            (tt * 4 + t4 + 1) * E],
                                       prb[:, 0:8], prb[:, 8:16], ALU.mult),
                                  s["vec"])
                    if P2STAGE < 6:
                        return
                    c.wait("PE", s["vec"])
                    for k in range(TLOC // 128):
                        if k >= 1:
                            c.wait("PE", s["vec"])
                        c.inc(c.tr(psSt[0:8, 0:128], ctm[:, k * E:(k + 1) * E],
                                   idf[:]), s["pe"])
                        c.wait("VEC", s["pe"])
                        c.inc(c.vcopy(cfm[:, k * 128:(k + 1) * 128],
                                      psSt[0:8, 0:128]), s["vec"])
                    c.wait("SY", s["vec"])
                    c.inc(c.dma("SY", cfmh[:], cfm[:]), s["dS"], 16)
                    marks["p2_stores"] = s["dS"].c

                run_phase(nc, blk, sems, p2 if "p2" in PHASES else (lambda c: None))

        for hs2 in range(2):
            with nc.sbuf_tensor(f"acc{hs2}", [128, CCH * 2048], F32) as acc:
                with contextlib.ExitStack() as es3:
                    ec = es3.enter_context
                    x1b = ec(nc.sbuf_tensor(f"x1b{hs2}", [128, CCH * 2048], BF16))
                    hbf = ec(nc.sbuf_tensor(f"hbf{hs2}", [128, HC2 * 512], BF16))
                    cfsl = ec(nc.sbuf_tensor(f"cfsl{hs2}", [1, 2 * 512], F32))
                    sb2e = ec(nc.sbuf_tensor(f"sb2e{hs2}", [1, C], F32))
                    w1s = ec(nc.sbuf_tensor(f"w1s{hs2}", [128, CCH * 2048], BF16))
                    w2s = ec(nc.sbuf_tensor(f"w2s{hs2}", [128, HC2 * C], BF16))
                    crp = ec(nc.sbuf_tensor(f"crp{hs2}", [128, 2048], F32))
                    psH = ec(nc.psum_tensor(f"psH{hs2}", [128, 2048], F32))
                    psY = ec(nc.psum_tensor(f"psY{hs2}", [128, 2048], F32))
                    blk = ec(nc.Block())
                    def p3a(c: Ctx, hs2=hs2, acc=acc, x1b=x1b, hbf=hbf,
                            cfsl=cfsl, sb2e=sb2e, w1s=w1s, w2s=w2s, crp=crp,
                            psH=psH, psY=psY):
                        s = c.s
                        c.barrier("pe", "act", "vec")
                        c.wait("SY", s["dS"], marks.get("p2_stores", 0))
                        accd = acc[:].rearrange("p (c t) -> p c t", c=CCH)
                        spfs = (spf[:, hs2 * 2048:(hs2 + 1) * 2048]
                                .rearrange("(c p) t -> p c t", p=128))
                        x1bd = x1b[:].rearrange("p (c t) -> p c t", c=CCH)
                        spbs = (spb[:, hs2 * 2048:(hs2 + 1) * 2048]
                                .rearrange("(c p) t -> p c t", p=128))
                        for lo, hi in ((0, 4), (4, 8)):
                            c.inc(c.dma("SY", accd[:, lo:hi, :],
                                        spfs[:, lo:hi, :]), s["dA"], 16)
                            c.inc(c.dma("SY", x1bd[:, lo:hi, :],
                                        spbs[:, lo:hi, :]), s["dA"], 16)
                        prevA = 0
                        neh = [0]
                        for e in range(E):
                            if neh[0] >= P3EHS:
                                break
                            c.wait("SY", s["pe"])
                            c.inc(c.dma("SY", sb2e[:], b2[e:e + 1, :]),
                                  s["dA"], 16)
                            for hh in range(2):
                                if neh[0] >= P3EHS:
                                    break
                                neh[0] += 1
                                if P3STAGE < 2:
                                    continue
                                c.wait("GP", s["pe"], prevA)
                                w1d = w1s[:].rearrange("p (c h) -> p c h",
                                                       c=CCH)
                                w1src = (w1[e][:, hh * 2048:(hh + 1) * 2048]
                                         .rearrange("(c p) h -> p c h", p=128))
                                for lo, hi in ((0, 4), (4, 8)):
                                    c.inc(c.dma("GP", w1d[:, lo:hi, :],
                                                w1src[:, lo:hi, :]),
                                          s["dW1"], 16)
                                c.wait("GP", s["pe"])
                                w2d = w2s[:].rearrange("p (h d) -> p h d",
                                                       h=HC2)
                                w2src = (w2[e][hh * 2048:(hh + 1) * 2048, :]
                                         .rearrange("(h p) d -> p h d", p=128))
                                for lo, hi in ((0, 4), (4, 8), (8, 12),
                                               (12, 16)):
                                    c.inc(c.dma("GP", w2d[:, lo:hi, :],
                                                w2src[:, lo:hi, :]),
                                          s["dW2"], 16)
                                dW1_tgt = s["dW1"].c
                                dW2_tgt = s["dW2"].c
                                cf_tgt = {}
                                c.wait("SY", s["pe"])
                                c.inc(c.dma("SY", cfsl[:, 0:512],
                                            cfmh[e:e + 1,
                                                 hs2 * 2048:hs2 * 2048 + 512]),
                                      s["dA"], 16)
                                cf_tgt[0] = s["dA"].c
                                if hh == 0:
                                    crp_ready = {}
                                    crp_all = {}
                                for sl in range(NSLAB):
                                    if P3STAGE < 3:
                                        continue
                                    cfv = cfsl[:, (sl % 2) * 512:
                                               (sl % 2) * 512 + 512]
                                    if hh == 1:
                                        crp_done = crp_all[sl]
                                        crpv = crp[:, sl * 512:
                                                   (sl + 1) * 512]
                                    if hh == 1:
                                        pass
                                    elif sl + 1 < NSLAB:
                                        c.wait("SY", s["pe"])
                                        c.inc(c.dma(
                                            "SY",
                                            cfsl[:, ((sl + 1) % 2) * 512:
                                                 ((sl + 1) % 2) * 512 + 512],
                                            cfmh[e:e + 1,
                                                 hs2 * 2048 + (sl + 1) * 512:
                                                 hs2 * 2048 + (sl + 2) * 512]),
                                            s["dA"], 16)
                                        cf_tgt[sl + 1] = s["dA"].c
                                    if hh == 0:
                                        if sl == 0:
                                            c.wait("PE", s["dA"], cf_tgt[0])
                                            c.wait("PE", s["vec"])
                                            c.inc(c.mm(psH[:, 0:512], ones1[:],
                                                       cfv), s["pe"])
                                            c.wait("VEC", s["pe"])
                                            c.inc(c.vcopy(crp[:, 0:512],
                                                          psH[:, 0:512]),
                                                  s["vec"])
                                            crp_ready[0] = s["vec"].c
                                        crp_done = crp_ready[sl]
                                        crp_all[sl] = crp_done
                                        crpv = crp[:, sl * 512:
                                                   (sl + 1) * 512]
                                    if P3STAGE < 4:
                                        continue
                                    c.wait("PE", s["dW1"], dW1_tgt)
                                    evh = {}
                                    scl = s["vec"].c
                                    for hc in range(HC2):
                                        if hc == 0:
                                            c.wait("PE", s["vec"], crp_done)
                                        if hc >= 4:
                                            c.wait("PE", s["vec"], evh[hc - 4])
                                        for ci in range(CCH):
                                            ins = c.mm(
                                                psH[:, (hc % 4) * 512:
                                                    (hc % 4) * 512 + 512],
                                                w1s[:, ci * 2048 + hc * 128:
                                                    ci * 2048 + (hc + 1) * 128],
                                                x1b[:, ci * 2048 + sl * 512:
                                                    ci * 2048 + (sl + 1) * 512],
                                                start=(ci == 0),
                                                stop=(ci == CCH - 1))
                                        c.inc(ins, s["pe"])
                                        c.wait("VEC", s["pe"])
                                        c.inc(c.ts(
                                            hbf[:, hc * 512:(hc + 1) * 512],
                                            psH[:, (hc % 4) * 512:
                                                (hc % 4) * 512 + 512],
                                            sb1[:, e * 32 + hh * 16 + hc:
                                                e * 32 + hh * 16 + hc + 1],
                                            None, ALU.add), s["vec"])
                                        evh[hc] = s["vec"].c
                                        if hc == 7 and sl + 1 < NSLAB \
                                                and hh == 0 and P3STAGE >= 3:
                                            c.wait("PE", s["dA"],
                                                   cf_tgt[sl + 1])
                                            c.inc(c.mm(
                                                psY[:, 1536:2048], ones1[:],
                                                cfsl[:, ((sl + 1) % 2) * 512:
                                                     ((sl + 1) % 2) * 512
                                                     + 512]), s["pe"])
                                            c.wait("VEC", s["pe"])
                                            c.inc(c.vcopy(
                                                crp[:, (sl + 1) * 512:
                                                    (sl + 2) * 512],
                                                psY[:, 1536:2048]), s["vec"])
                                            crp_ready[sl + 1] = s["vec"].c
                                        if P3STAGE >= 5 and hc % 4 == 3:
                                            q0 = (hc - 3) * 512
                                            if hc == 15:
                                                c.wait("ACT", s["vec"],
                                                       evh[13])
                                                c.inc(c.act(
                                                    hbf[:, q0:q0 + 1024],
                                                    hbf[:, q0:q0 + 1024],
                                                    AF.Gelu), s["act"])
                                                gq = s["act"].c
                                                c.wait("VEC", s["act"], gq)
                                                c.wait("VEC", s["vec"],
                                                       crp_done)
                                                for k4 in range(2):
                                                    qq = q0 + k4 * 512
                                                    c.inc(c.tt(
                                                        hbf[:, qq:qq + 512],
                                                        hbf[:, qq:qq + 512],
                                                        crpv, ALU.mult),
                                                        s["vec"])
                                                c.wait("ACT", s["vec"],
                                                       evh[15])
                                                c.inc(c.act(
                                                    hbf[:, q0 + 1024:
                                                        q0 + 2048],
                                                    hbf[:, q0 + 1024:
                                                        q0 + 2048],
                                                    AF.Gelu), s["act"])
                                                gq = s["act"].c
                                                c.wait("VEC", s["act"], gq)
                                                for k4 in range(2, 4):
                                                    qq = q0 + k4 * 512
                                                    c.inc(c.tt(
                                                        hbf[:, qq:qq + 512],
                                                        hbf[:, qq:qq + 512],
                                                        crpv, ALU.mult),
                                                        s["vec"])
                                                scl = s["vec"].c
                                            else:
                                                c.wait("ACT", s["vec"],
                                                       evh[hc])
                                                c.inc(c.act(
                                                    hbf[:, q0:q0 + 2048],
                                                    hbf[:, q0:q0 + 2048],
                                                    AF.Gelu), s["act"])
                                                gq = s["act"].c
                                                c.wait("VEC", s["act"], gq)
                                                c.wait("VEC", s["vec"],
                                                       crp_done)
                                                for k4 in range(4):
                                                    qq = q0 + k4 * 512
                                                    c.inc(c.tt(
                                                        hbf[:, qq:qq + 512],
                                                        hbf[:, qq:qq + 512],
                                                        crpv, ALU.mult),
                                                        s["vec"])
                                                scl = s["vec"].c
                                    if sl == NSLAB - 1:
                                        prevA = s["pe"].c
                                    if P3STAGE < 5:
                                        continue
                                    if P3STAGE < 6:
                                        continue
                                    c.wait("PE", s["dW2"], dW2_tgt)
                                    c.wait("PE", s["vec"], scl)
                                    evy = {}
                                    for dc in range(CCH):
                                        yb = psY[:, (dc % 4) * 512:
                                                 (dc % 4) * 512 + 512]
                                        if dc >= 4:
                                            c.wait("PE", s["vec"], evy[dc - 4])
                                        first = True
                                        if hh == 0:
                                            ins = c.mm(
                                                yb,
                                                sb2e[:, dc * 128:(dc + 1) * 128],
                                                cfv, start=True, stop=False)
                                            first = False
                                        for hc in range(HC2):
                                            ins = c.mm(
                                                yb,
                                                w2s[:, hc * C + dc * 128:
                                                    hc * C + (dc + 1) * 128],
                                                hbf[:, hc * 512:(hc + 1) * 512],
                                                start=first and (hc == 0),
                                                stop=(hc == HC2 - 1))
                                            first = False
                                        c.inc(ins, s["pe"])
                                        c.wait("VEC", s["pe"])
                                        c.wait("VEC", s["dA"])
                                        c.inc(c.stt(
                                            acc[:, dc * 2048 + sl * 512:
                                                dc * 2048 + (sl + 1) * 512],
                                            yb, 1.0,
                                            acc[:, dc * 2048 + sl * 512:
                                                dc * 2048 + (sl + 1) * 512],
                                            ALU.mult, ALU.add), s["vec"])
                                        evy[dc] = s["vec"].c

                    run_phase(nc, blk, sems,
                              p3a if "p3a" in PHASES else (lambda c: None))

                with contextlib.ExitStack() as es4:
                    ec = es4.enter_context
                    ld2 = ec(nc.sbuf_tensor(f"ld2{hs2}", [128, CCH * 512], F32))
                    osg = ec(nc.sbuf_tensor(f"osg{hs2}", [128, 4 * C], F32))
                    msb2 = ec(nc.sbuf_tensor(f"msb2{hs2}", [128, 512], F32))
                    mrw = ec(nc.sbuf_tensor(f"mrw{hs2}", [1, 512], F32))
                    sq3 = ec(nc.sbuf_tensor(f"sq3{hs2}", [128, 2 * 512], F32))
                    psS2 = ec(nc.psum_tensor(f"psS2{hs2}", [128, 512], F32))
                    psM2 = ec(nc.psum_tensor(f"psM2{hs2}", [128, 512], F32))
                    psT2 = ec(nc.psum_tensor(f"psT2{hs2}", [128, 1024], F32))
                    blk = ec(nc.Block())
                    def p3b(c: Ctx, hs2=hs2, acc=acc, ld2=ld2, osg=osg,
                            msb2=msb2, mrw=mrw, sq3=sq3, psS2=psS2,
                            psM2=psM2, psT2=psT2):
                        s = c.s
                        c.barrier("pe", "act", "vec")
                        for sl in range(NSLAB):
                            def asl(ci):
                                return acc[:, ci * 2048 + sl * 512:
                                           ci * 2048 + (sl + 1) * 512]
                            if P3BSTAGE < 1:
                                continue
                            c.wait("PE", s["vec"])
                            for ci in range(CCH):
                                ins = c.mm(psS2[0:1, 0:512], onescol[:],
                                           asl(ci), start=(ci == 0),
                                           stop=(ci == CCH - 1))
                            c.inc(ins, s["pe"])
                            c.wait("VEC", s["pe"])
                            c.inc(c.ts(mrw[:], psS2[0:1, 0:512], 1.0 / C, None,
                                       ALU.mult), s["vec"])
                            c.wait("PE", s["vec"])
                            c.inc(c.mm(psM2[:], ones1[:], mrw[:]), s["pe"])
                            c.wait("VEC", s["pe"])
                            c.inc(c.vcopy(msb2[:], psM2[:]), s["vec"])
                            c.wait("VEC", s["vec"])
                            for ci in range(CCH):
                                c.inc(c.tt(ld2[:, ci * 512:(ci + 1) * 512],
                                           asl(ci), msb2[:], ALU.subtract),
                                      s["vec"])
                            d_done = s["vec"].c
                            if P3BSTAGE < 2:
                                continue
                            sqmm = {}
                            for ci in range(CCH):
                                c.wait("ACT", s["vec"], d_done)
                                if ci >= 2:
                                    c.wait("ACT", s["pe"], sqmm[ci - 2])
                                c.inc(c.act(sq3[:, (ci % 2) * 512:
                                                (ci % 2) * 512 + 512],
                                            ld2[:, ci * 512:(ci + 1) * 512],
                                            AF.Square), s["act"])
                                c.wait("PE", s["act"])
                                c.inc(c.mm(psS2[0:1, 0:512], onescol[:],
                                           sq3[:, (ci % 2) * 512:
                                               (ci % 2) * 512 + 512],
                                           start=(ci == 0),
                                           stop=(ci == CCH - 1)), s["pe"])
                                sqmm[ci] = s["pe"].c
                            c.wait("VEC", s["pe"])
                            c.inc(c.ts(mrw[:], psS2[0:1, 0:512], 1.0 / C, None,
                                       ALU.mult), s["vec"])
                            c.wait("ACT", s["vec"])
                            c.inc(c.act(mrw[:], mrw[:], AF.Ln,
                                        bias=kc[0:1, 0:1]), s["act"])
                            c.wait("ACT", s["act"])
                            c.inc(c.act(mrw[:], mrw[:], AF.Exp,
                                        scale=kc[0:1, 1:2]), s["act"])
                            c.wait("PE", s["act"])
                            c.inc(c.mm(psM2[:], ones1[:], mrw[:]), s["pe"])
                            c.wait("VEC", s["pe"])
                            c.inc(c.vcopy(msb2[:], psM2[:]), s["vec"])
                            c.wait("VEC", s["vec"])
                            if P3BSTAGE < 3:
                                continue
                            for ci in range(CCH):
                                c.inc(c.tt(ld2[:, ci * 512:(ci + 1) * 512],
                                           ld2[:, ci * 512:(ci + 1) * 512],
                                           msb2[:], ALU.mult), s["vec"])
                            nrm = s["vec"].c
                            c.wait("PE", s["vec"], nrm)
                            if P3BSTAGE < 4:
                                continue
                            c.wait("PE", s["dS"])
                            cpd = {}
                            for t4 in range(4):
                                for ci in range(CCH):
                                    idx = t4 * CCH + ci
                                    slot = (idx % 2) * 512
                                    if idx >= 2:
                                        c.wait("PE", s["vec"], cpd[idx - 2])
                                    c.inc(c.tr(psT2[:, slot:slot + 128],
                                               ld2[:, ci * 512 + t4 * 128:
                                                   ci * 512 + (t4 + 1) * 128],
                                               idf[:]), s["pe"])
                                    c.wait("VEC", s["pe"])
                                    c.inc(c.vcopy(
                                        osg[:, t4 * C + ci * 128:
                                            t4 * C + (ci + 1) * 128],
                                        psT2[:, slot:slot + 128]),
                                        s["vec"])
                                    cpd[idx] = s["vec"].c
                            if P3BSTAGE < 5:
                                continue
                            c.wait("SY", s["vec"])
                            c.inc(c.dma("SY",
                                        out[hs2 * 2048 + sl * 512:
                                            hs2 * 2048 + (sl + 1) * 512, :]
                                        .rearrange("(x p) c2 -> p x c2", p=128),
                                        osg[:].rearrange("p (x c2) -> p x c2",
                                                         x=4)),
                                  s["dS"], 16)
                        if hs2 == 1:
                            c.wait("SY", s["dS"])

                    run_phase(nc, blk, sems,
                              p3b if "p3b" in PHASES else (lambda c: None))

    return nc


_CACHED = {}


def kernel(L0_attn_x, L0_attn_Wqkv, L0_attn_bqkv, L0_attn_r_sigma,
           L0_attn_Wproj, L0_attn_bproj, L0_ffn_Wr, L0_ffn_W1, L0_ffn_b1,
           L0_ffn_W2, L0_ffn_b2):
    x = np.asarray(L0_attn_x, dtype=np.float32)
    wqkv = np.ascontiguousarray(np.asarray(L0_attn_Wqkv, dtype=np.float32))
    bqkv = np.asarray(L0_attn_bqkv, dtype=np.float32)
    sig = np.asarray(L0_attn_r_sigma, dtype=np.float32).reshape(1, 1)
    wproj = np.ascontiguousarray(np.asarray(L0_attn_Wproj, dtype=np.float32))
    bproj = np.asarray(L0_attn_bproj, dtype=np.float32)
    wr = np.asarray(L0_ffn_Wr, dtype=np.float32)
    w1 = np.ascontiguousarray(np.asarray(L0_ffn_W1, dtype=np.float32))
    b1 = np.asarray(L0_ffn_b1, dtype=np.float32)
    w2 = np.ascontiguousarray(np.asarray(L0_ffn_W2, dtype=np.float32))
    b2 = np.asarray(L0_ffn_b2, dtype=np.float32)

    bqkvT = np.ascontiguousarray(bqkv.reshape(24, 128).T)
    bprojT = np.ascontiguousarray(bproj.reshape(8, 128).T)
    wrT = np.ascontiguousarray(wr.T)
    b1T = np.ascontiguousarray(
        np.stack([b1[e].reshape(32, 128).T for e in range(E)], axis=1)
        .reshape(128, E * 32))

    if "nc" not in _CACHED:
        _CACHED["nc"] = build_program()
    nc = _CACHED["nc"]

    in_maps = []
    xTs = {}
    for core in range(NCORE):
        b = core // 2
        hs = core % 2
        key = (b, hs)
        if key not in xTs:
            xt = np.ascontiguousarray(x[b].T)
            if hs == 1:
                xt = np.ascontiguousarray(np.roll(xt, -4096, axis=1))
            xTs[key] = xt
        in_maps.append({
            "xT": xTs[key], "wqkv": wqkv, "bqkvT": bqkvT, "sigma": sig,
            "wproj": wproj, "bprojT": bprojT, "wrT": wrT, "w1": w1,
            "b1T": b1T, "w2": w2, "b2": b2,
        })

    res = run_bass_kernel_spmd(nc, in_maps, core_ids=list(range(NCORE)))
    outp = np.empty((B, T, C), dtype=np.float32)
    for core in range(NCORE):
        b = core // 2
        hs = core % 2
        outp[b, hs * 4096:(hs + 1) * 4096, :] = res.results[core]["out"]
    return outp


# version 10
# speedup vs baseline: 1.0679x; 1.0679x over previous
import numpy as np
import contextlib
import concourse.bass as bass
import concourse.mybir as mybir
from concourse.bass_utils import run_bass_kernel_spmd

F32 = mybir.dt.float32
BF16 = mybir.dt.bfloat16
AF = mybir.ActivationFunctionType
ALU = mybir.AluOpType
AX = mybir.AxisListType

B, T, C = 4, 8192, 1024
NH, HD = 16, 64
E, HFF = 8, 4096
NCORE = 8
TLOC = 4096
CCH = 8
NPAIR = 8
NTT = 16
JC = 64
NIB = 4
HC2 = 16
NSLAB = 4

ENGS = ("PE", "ACT", "VEC", "GP", "SY")
PHASES = {"p0", "p1", "p2", "p3a", "p3b"}
P0STAGE = 99
P1PAIRS = 99
P1STAGE = 99
P1IBS = 99
P1JCS = 99
P2TTS = 99
P2STAGE = 99
P3EHS = 99
P3STAGE = 99
P3BSTAGE = 99


class Sem:
    def __init__(self, handle):
        self.h = handle
        self.c = 0


class Ctx:
    def __init__(self, nc, key, eng, sems):
        self.nc = nc
        self.key = key
        self.eng = eng
        self.s = sems

    def inc(self, ins, sem, n=1):
        sem.c += n
        if ins is not None:
            ins.then_inc(sem.h, n)
        return ins

    def wait(self, key, sem, val=None):
        if self.key == key:
            self.eng.wait_ge(sem.h, sem.c if val is None else val)

    def barrier(self, *semnames):
        for key in ("PE", "ACT", "VEC"):
            for sn in semnames:
                if sn in ("pe",) and key == "PE":
                    continue
                if sn in ("act",) and key == "ACT":
                    continue
                if sn in ("vec",) and key == "VEC":
                    continue
                self.wait(key, self.s[sn])

    def mm(self, out, lhsT, rhs, start=True, stop=True, tp=None, sgc=False):
        if self.key == "PE":
            return self.nc.tensor.matmul(out, lhsT, rhs, start=start, stop=stop,
                                         tile_position=tp, skip_group_check=sgc)
        return None

    def tr(self, out, in_, ident):
        if self.key == "PE":
            return self.nc.tensor.transpose(out, in_, ident)
        return None

    def act(self, out, in_, fn, bias=0.0, scale=1.0):
        if self.key == "ACT":
            return self.nc.scalar.activation(out, in_, fn, bias=bias, scale=scale)
        return None

    def ts(self, out, in0, s1, s2, op0, op1=None):
        if self.key == "VEC":
            if op1 is None:
                return self.nc.vector.tensor_scalar(out, in0, s1, None, op0)
            return self.nc.vector.tensor_scalar(out, in0, s1, s2, op0, op1)
        return None

    def tt(self, out, in0, in1, op):
        if self.key == "VEC":
            return self.nc.vector.tensor_tensor(out, in0, in1, op)
        return None

    def stt(self, out, in0, s, in1, op0, op1):
        if self.key == "VEC":
            return self.nc.vector.scalar_tensor_tensor(out, in0, s, in1, op0, op1)
        return None

    def vcopy(self, out, in_):
        if self.key == "VEC":
            return self.nc.vector.tensor_copy(out, in_)
        return None

    def vred(self, out, in_, op):
        if self.key == "VEC":
            return self.nc.vector.tensor_reduce(out, in_, AX.X, op)
        return None

    def vrecip(self, out, in_):
        if self.key == "VEC":
            return self.nc.vector.reciprocal(out, in_)
        return None

    def dma(self, q, out, in_):
        if self.key == q:
            if q == "GP":
                return self.nc.gpsimd.dma_start(out, in_)
            return self.nc.sync.dma_start(out, in_)
        return None

    def memset(self, ap, v):
        if self.key == "GP":
            return self.nc.gpsimd.memset(ap, v)
        return None

    def gnop(self):
        if self.key == "GP":
            return self.nc.gpsimd.nop()
        return None


def run_phase(nc, blk, sems, fn):
    snap = {k: s.c for k, s in sems.items()}
    finals = []
    engmap = {"PE": blk.tensor, "ACT": blk.scalar, "VEC": blk.vector,
              "GP": blk.gpsimd, "SY": blk.sync}
    for key in ENGS:
        for k, s in sems.items():
            s.c = snap[k]
        def body(eng, key=key):
            fn(Ctx(nc, key, eng, sems))
        engmap[key](body)
        finals.append({k: s.c for k, s in sems.items()})
    for f in finals[1:]:
        assert f == finals[0], f"engine walk divergence: {finals[0]} vs {f}"


DETECT_RACES = True


def build_program():
    nc = bass.Bass("TRN2", target_bir_lowering=False,
                   detect_race_conditions=DETECT_RACES)

    xT = nc.dram_tensor("xT", [C, T], F32, kind="ExternalInput").ap()
    wqkv = nc.dram_tensor("wqkv", [C, 3 * C], F32, kind="ExternalInput").ap()
    bqkvT = nc.dram_tensor("bqkvT", [128, 24], F32, kind="ExternalInput").ap()
    sigma = nc.dram_tensor("sigma", [1, 1], F32, kind="ExternalInput").ap()
    wproj = nc.dram_tensor("wproj", [C, C], F32, kind="ExternalInput").ap()
    bprojT = nc.dram_tensor("bprojT", [128, 8], F32, kind="ExternalInput").ap()
    wrT = nc.dram_tensor("wrT", [C, E], F32, kind="ExternalInput").ap()
    w1 = nc.dram_tensor("w1", [E, C, HFF], F32, kind="ExternalInput").ap()
    b1T = nc.dram_tensor("b1T", [128, 256], F32, kind="ExternalInput").ap()
    w2 = nc.dram_tensor("w2", [E, HFF, C], F32, kind="ExternalInput").ap()
    b2 = nc.dram_tensor("b2", [E, C], F32, kind="ExternalInput").ap()
    out = nc.dram_tensor("out", [TLOC, C], F32, kind="ExternalOutput").ap()
    spf = nc.dram_tensor("spf", [C, TLOC], F32, kind="Internal").ap()
    spb = nc.dram_tensor("spb", [C, TLOC], BF16, kind="Internal").ap()
    cfmh = nc.dram_tensor("cfmh", [E, TLOC], F32, kind="Internal").ap()

    marks = {}
    stack = contextlib.ExitStack()
    with stack:
        sems = {}
        for name in ("dX", "dWp", "dW1", "dW2", "dA", "dS",
                     "pe", "act", "vec", "gp"):
            sems[name] = Sem(stack.enter_context(nc.semaphore(name)))

        kc = stack.enter_context(nc.sbuf_tensor("kc", [128, 4], F32))
        ones1 = stack.enter_context(nc.sbuf_tensor("ones1", [1, 128], F32))
        onescol = stack.enter_context(nc.sbuf_tensor("onescol", [128, 1], F32))
        mask64 = stack.enter_context(nc.sbuf_tensor("mask64", [128, 2], F32))
        ind2 = stack.enter_context(nc.sbuf_tensor("ind2", [2, 128], F32))
        idb = stack.enter_context(nc.sbuf_tensor("idb", [128, 128], BF16))
        idf = stack.enter_context(nc.sbuf_tensor("idf", [128, 128], F32))
        ssg = stack.enter_context(nc.sbuf_tensor("ssg", [1, 1], F32))
        sgc = stack.enter_context(nc.sbuf_tensor("sgc", [128, 5], F32))
        sbq = stack.enter_context(nc.sbuf_tensor("sbq", [128, 24], F32))
        sbp = stack.enter_context(nc.sbuf_tensor("sbp", [128, 8], F32))
        sb1 = stack.enter_context(nc.sbuf_tensor("sb1", [128, 256], F32))
        swr = stack.enter_context(nc.sbuf_tensor("swr", [128, CCH * E], F32))
        ctm = stack.enter_context(
            nc.sbuf_tensor("ctm", [128, (TLOC // 128) * E], F32))

        with nc.psum_tensor("ps0", [128, 128], F32) as ps0, nc.Block() as blk:
            def p0(c: Ctx):
                s = c.s
                for ap, v in ((kc[:, 0:1], 1e-5), (kc[:, 1:2], -0.5),
                              (ones1[:], 1.0), (onescol[:], 1.0),
                              (mask64[:], 0.0), (idb[:], 0.0), (idf[:], 0.0)):
                    c.inc(c.memset(ap, v), s["gp"])
                c.wait("GP", s["gp"])
                c.inc(c.memset(mask64[0:64, 0:1], 1.0), s["gp"])
                c.inc(c.memset(mask64[64:128, 1:2], 1.0), s["gp"])
                for ident in (idb, idf):
                    if c.key == "GP":
                        ins = nc.gpsimd.affine_select(
                            out=ident[:], in_=ident[:], compare_op=ALU.not_equal,
                            fill=1.0, base=0, pattern=[[-1, 128]],
                            channel_multiplier=1)
                    else:
                        ins = None
                    c.inc(ins, s["gp"])
                if P0STAGE < 2:
                    return
                c.inc(c.dma("SY", ssg[:], sigma[:]), s["dA"], 16)
                c.inc(c.dma("SY", sbq[:], bqkvT[:]), s["dA"], 16)
                c.inc(c.dma("SY", sbp[:], bprojT[:]), s["dA"], 16)
                c.inc(c.dma("SY", sb1[:], b1T[:]), s["dA"], 16)
                c.inc(c.dma("SY", swr[:].rearrange("p (c e) -> p c e", e=E),
                            wrT.rearrange("(c p) e -> p c e", p=128)),
                      s["dA"], 16)
                if P0STAGE < 3:
                    return
                c.wait("PE", s["dA"])
                c.wait("PE", s["gp"])
                c.inc(c.mm(ps0[:, 0:1], ones1[:], ssg[:]), s["pe"])
                c.wait("VEC", s["pe"])
                c.inc(c.vcopy(sgc[:, 0:1], ps0[:, 0:1]), s["vec"])
                c.wait("VEC", s["vec"])
                c.inc(c.ts(sgc[:, 1:2], sgc[:, 0:1], 2.0, None, ALU.mult),
                      s["vec"])
                c.inc(c.ts(sgc[:, 2:3], sgc[:, 0:1], -1.0, None, ALU.mult),
                      s["vec"])
                c.inc(c.ts(sgc[:, 3:4], sgc[:, 0:1], -64.0, None, ALU.mult),
                      s["vec"])
                c.inc(c.ts(sgc[:, 4:5], sgc[:, 0:1], 64.0, None, ALU.mult),
                      s["vec"])
                if P0STAGE < 4:
                    return
                c.wait("PE", s["vec"])
                c.inc(c.tr(ps0[0:2, 0:128], mask64[:], idf[:]), s["pe"])
                c.wait("VEC", s["pe"])
                c.inc(c.vcopy(ind2[:], ps0[0:2, 0:128]), s["vec"])

            run_phase(nc, blk, sems, p0 if "p0" in PHASES else (lambda c: None))

        with nc.sbuf_tensor("ot", [128, NPAIR * TLOC], BF16) as ot:
            with contextlib.ExitStack() as es1:
                ec = es1.enter_context
                xst = ec(nc.sbuf_tensor("xst", [128, 2 * CCH * 512], BF16))
                wqsb = ec(nc.sbuf_tensor("wqsb", [128, CCH * 128], BF16))
                wksb = ec(nc.sbuf_tensor("wksb", [128, CCH * 128], BF16))
                wvsb = ec(nc.sbuf_tensor("wvsb", [128, CCH * 128], BF16))
                qt = ec(nc.sbuf_tensor("qt", [128, T], BF16))
                vtA = ec(nc.sbuf_tensor("vtA", [128, T], BF16))
                vtB = ec(nc.sbuf_tensor("vtB", [128, T], BF16))
                ktA = ec(nc.sbuf_tensor("ktA", [128, TLOC], BF16))
                ktB = ec(nc.sbuf_tensor("ktB", [128, TLOC], BF16))
                vbb = ec(nc.sbuf_tensor("vbb", [128, 1024], BF16))
                pt = ec(nc.sbuf_tensor("pt", [128, 2 * 2048], BF16))
                qnrow = ec(nc.sbuf_tensor("qnrow", [2, 512], F32))
                qnT = ec(nc.sbuf_tensor("qnT", [128, 128], F32))
                biasm = ec(nc.sbuf_tensor("biasm", [128, 128], F32))
                knfm = ec(nc.sbuf_tensor("knfm", [2, TLOC], F32))
                rhat = ec(nc.sbuf_tensor("rhat", [128, 1024], F32))
                sqv = ec(nc.sbuf_tensor("sqv", [128, 4 * 512], F32))
                psS = ec(nc.psum_tensor("psS", [128, 2048], F32))
                psO = ec(nc.psum_tensor("psO", [128, 1024], F32))
                psT = ec(nc.psum_tensor("psT", [128, 1024], BF16))
                psX = ec(nc.psum_tensor("psX", [128, 512], F32))
                blk = ec(nc.Block())
                def p1(c: Ctx):
                    s = c.s
                    c.inc(c.memset(ktA[64:128, :], 0.0), s["gp"])
                    c.inc(c.memset(ktB[0:64, :], 0.0), s["gp"])
                    c.inc(c.memset(vtA[:], 0.0), s["gp"])
                    c.inc(c.memset(vtB[:], 0.0), s["gp"])
                    c.wait("PE", s["gp"])
                    qkv_prev = 0
                    for p in range(min(NPAIR, P1PAIRS)):
                        c.barrier("pe", "act", "vec", "gp")
                        c.wait("GP", s["pe"], qkv_prev)
                        for wsb, off in ((wqsb, 0), (wksb, C), (wvsb, 2 * C)):
                            wd = wsb[:].rearrange("p (c m) -> p c m", c=CCH)
                            ws = (wqkv[:, off + p * 128:off + (p + 1) * 128]
                                  .rearrange("(c p) m -> p c m", p=128))
                            c.inc(c.dma("GP", wd[:, 0:4, :], ws[:, 0:4, :]),
                                  s["dWp"], 16)
                            c.inc(c.dma("GP", wd[:, 4:8, :], ws[:, 4:8, :]),
                                  s["dWp"], 16)
                        mm_done = {}
                        ev_sq = {}

                        def qkv_banks(tt):
                            if tt % 2 == 0:
                                return (psS[:, 0:512], psS[:, 512:1024],
                                        psS[:, 1024:1536])
                            return (psS[:, 1536:2048], psO[:, 0:512],
                                    psO[:, 512:1024])

                        def emit_tail(tt):
                            local = tt < 8
                            bq, bv, bk = qkv_banks(tt)
                            c.wait("PE", s["act"], ev_sq[tt][1])
                            c.wait("PE", s["vec"])
                            c.inc(c.mm(psX[0:2, 0:512], mask64[:],
                                       sqv[:, 0:512] if tt % 2 == 0 else
                                       sqv[:, 1024:1536]), s["pe"])
                            c.wait("VEC", s["pe"])
                            c.inc(c.vcopy(qnrow[:], psX[0:2, 0:512]), s["vec"])
                            qnrow_done = s["vec"].c
                            if local:
                                c.wait("PE", s["vec"], qnrow_done)
                                c.inc(c.mm(psX[0:2, 0:512], mask64[:],
                                           sqv[:, 512:1024] if tt % 2 == 0
                                           else sqv[:, 1536:2048]), s["pe"])
                                c.wait("VEC", s["pe"])
                                c.inc(c.vcopy(knfm[0:2, tt * 512:
                                              (tt + 1) * 512],
                                              psX[0:2, 0:512]), s["vec"])
                            c.wait("PE", s["vec"])
                            for k in range(4):
                                ins = c.tr(psT[:, k * 128:(k + 1) * 128],
                                           vbb[:, (tt % 2) * 512 + k * 128:
                                               (tt % 2) * 512 + (k + 1) * 128],
                                           idb[:])
                            c.inc(ins, s["pe"])
                            c.wait("VEC", s["pe"])
                            for k in range(4):
                                jcg = tt * 4 + k
                                c.inc(c.vcopy(
                                    vtA[:, jcg * 128:jcg * 128 + 64],
                                    psT[:, k * 128:k * 128 + 64]), s["vec"])
                                c.inc(c.vcopy(
                                    vtB[:, jcg * 128 + 64:jcg * 128 + 128],
                                    psT[:, k * 128 + 64:k * 128 + 128]),
                                    s["vec"])
                            c.wait("PE", s["vec"], qnrow_done)
                            for k in range(4):
                                ins = c.tr(psX[:, 256 + k * 2:258 + k * 2],
                                           qnrow[:, k * 128:(k + 1) * 128],
                                           idf[0:2, 0:2])
                            c.inc(ins, s["pe"])
                            c.wait("VEC", s["pe"])
                            c.inc(c.vcopy(qnT[:, tt * 8:tt * 8 + 8],
                                          psX[:, 256:264]), s["vec"])

                        for tt in range(NTT):
                            buf = tt % 2
                            xv = xst[:, buf * CCH * 512:(buf + 1) * CCH * 512]
                            if tt >= 2:
                                c.wait("GP", s["pe"], mm_done[tt - 2])
                            else:
                                c.wait("GP", s["pe"], qkv_prev)
                            xsrc = (xT[:, tt * 512:(tt + 1) * 512]
                                    .rearrange("(c p) t -> p c t", p=128))
                            xdst = xv.rearrange("p (c t) -> p c t", c=CCH)
                            c.inc(c.dma("GP", xdst[:, 0:4, :], xsrc[:, 0:4, :]),
                                  s["dX"], 16)
                            c.inc(c.dma("GP", xdst[:, 4:8, :], xsrc[:, 4:8, :]),
                                  s["dX"], 16)
                            local = tt < 8
                            bq, bv, bk = qkv_banks(tt)
                            c.wait("PE", s["dX"])
                            if tt == 0:
                                c.wait("PE", s["dWp"])
                            if tt >= 2:
                                c.wait("PE", s["vec"], ev_sq[tt - 2][0])
                                c.wait("PE", s["act"], ev_sq[tt - 2][1])
                            for cc in range(CCH):
                                ins = c.mm(bq, wqsb[:, cc * 128:(cc + 1) * 128],
                                           xv[:, cc * 512:(cc + 1) * 512],
                                           start=(cc == 0), stop=(cc == CCH - 1))
                            c.inc(ins, s["pe"])
                            for cc in range(CCH):
                                ins = c.mm(bv, wvsb[:, cc * 128:(cc + 1) * 128],
                                           xv[:, cc * 512:(cc + 1) * 512],
                                           start=(cc == 0), stop=(cc == CCH - 1))
                            c.inc(ins, s["pe"])
                            if local:
                                for cc in range(CCH):
                                    ins = c.mm(bk,
                                               wksb[:, cc * 128:(cc + 1) * 128],
                                               xv[:, cc * 512:(cc + 1) * 512],
                                               start=(cc == 0),
                                               stop=(cc == CCH - 1))
                                c.inc(ins, s["pe"])
                            mm_done[tt] = s["pe"].c
                            sq0 = 0 if tt % 2 == 0 else 1024
                            c.wait("ACT", s["pe"], mm_done[tt])
                            c.inc(c.act(sqv[:, sq0:sq0 + 512], bq, AF.Square,
                                        bias=sbq[:, p:p + 1]), s["act"])
                            if local:
                                c.inc(c.act(sqv[:, sq0 + 512:sq0 + 1024], bk,
                                            AF.Square,
                                            bias=sbq[:, 8 + p:9 + p]), s["act"])
                            sq_done = s["act"].c
                            c.wait("VEC", s["pe"], mm_done[tt])
                            c.wait("VEC", s["act"], sq_done)
                            c.inc(c.ts(qt[:, tt * 512:(tt + 1) * 512], bq,
                                       sbq[:, p:p + 1], None, ALU.add),
                                  s["vec"])
                            c.inc(c.ts(vbb[:, (tt % 2) * 512:
                                           (tt % 2) * 512 + 512], bv,
                                       sbq[:, 16 + p:17 + p], None, ALU.add),
                                  s["vec"])
                            if local:
                                c.inc(c.ts(ktA[0:64, tt * 512:(tt + 1) * 512],
                                           bk[0:64, :] if tt % 2 == 0 else
                                           psO[0:64, 512:1024],
                                           sbq[0:64, 8 + p:9 + p], None,
                                           ALU.add), s["vec"])
                                c.inc(c.ts(ktB[64:128, tt * 512:(tt + 1) * 512],
                                           bk[64:128, :] if tt % 2 == 0 else
                                           psO[64:128, 512:1024],
                                           sbq[64:128, 8 + p:9 + p], None,
                                           ALU.add), s["vec"])
                            ev_sq[tt] = (s["vec"].c, s["act"].c)
                            if tt >= 1:
                                emit_tail(tt - 1)
                        emit_tail(NTT - 1)
                        qkv_prev = s["pe"].c
                        if P1STAGE < 2:
                            continue
                        c.wait("VEC", s["vec"])
                        c.inc(c.ts(biasm[:], qnT[:], sgc[:, 2:3], sgc[:, 3:4],
                                   ALU.mult, ALU.add), s["vec"])
                        c.wait("ACT", s["vec"])
                        c.wait("ACT", s["pe"])
                        c.inc(c.act(knfm[:], knfm[:], AF.Exp,
                                    bias=sgc[0:2, 4:5], scale=sgc[0:2, 2:3]),
                              s["act"])
                        stats_act = s["act"].c
                        if P1STAGE < 3:
                            continue
                        for ib in range(min(NIB, P1IBS)):
                            c.wait("PE", s["act"], stats_act)
                            c.wait("PE", s["vec"])
                            for it in range(2):
                                if it == 1:
                                    c.wait("PE", s["vec"])
                                c.inc(c.mm(psX[:, 0:512], ind2[:],
                                           knfm[0:2,
                                                ib * 1024 + it * 512:
                                                ib * 1024 + (it + 1) * 512]),
                                      s["pe"])
                                c.wait("VEC", s["pe"])
                                c.inc(c.vcopy(rhat[:, it * 512:(it + 1) * 512],
                                              psX[:, 0:512]), s["vec"])
                            expdone = {}
                            odone = {}
                            sdone = {}

                            def emit_S(jc, head):
                                base = 0 if head == 0 else 1024
                                ktp = ktA if head == 0 else ktB
                                for it in range(2):
                                    ins = c.mm(
                                        psS[:, base + it * 512:
                                            base + (it + 1) * 512],
                                        qt[:, jc * 128:(jc + 1) * 128],
                                        ktp[:, ib * 1024 + it * 512:
                                            ib * 1024 + (it + 1) * 512])
                                c.inc(ins, s["pe"])
                                return s["pe"].c

                            def emit_O(jc, head):
                                pb = (jc % 2) * 2048 + head * 1024
                                vtp = vtA if head == 0 else vtB
                                for it in range(2):
                                    ins = c.mm(
                                        psO[:, it * 512:(it + 1) * 512],
                                        vtp[:, jc * 128:(jc + 1) * 128],
                                        pt[:, pb + it * 512:pb + (it + 1) * 512],
                                        start=(jc == 0 and head == 0),
                                        stop=(jc == min(JC, P1JCS) - 1
                                              and head == 1),
                                        sgc=True)
                                c.inc(ins, s["pe"])
                                return s["pe"].c

                            def emit_exp(jc, head):
                                pb = (jc % 2) * 2048 + head * 1024
                                base = 0 if head == 0 else 1024
                                ins = c.act(pt[:, pb:pb + 1024],
                                            psS[:, base:base + 1024], AF.Exp,
                                            bias=biasm[:, jc * 2 + head:
                                                       jc * 2 + head + 1],
                                            scale=sgc[:, 1:2])
                                c.inc(ins, s["act"])
                                return s["act"].c

                            sA = emit_S(0, 0)
                            sB = emit_S(0, 1)
                            sdone[0] = (sA, sB)
                            NJC = min(JC, P1JCS)
                            for jc in range(NJC):
                                c.wait("ACT", s["pe"], sdone[jc][0])
                                if jc >= 2:
                                    c.wait("ACT", s["pe"], odone[jc - 2][0])
                                eA = emit_exp(jc, 0)
                                c.wait("ACT", s["pe"], sdone[jc][1])
                                if jc >= 2:
                                    c.wait("ACT", s["pe"], odone[jc - 2][1])
                                eB = emit_exp(jc, 1)
                                expdone[jc] = (eA, eB)
                                c.wait("PE", s["act"], eA)
                                oA = emit_O(jc, 0)
                                if jc + 1 < NJC:
                                    nsA = emit_S(jc + 1, 0)
                                c.wait("PE", s["act"], eB)
                                oB = emit_O(jc, 1)
                                if jc + 1 < NJC:
                                    nsB = emit_S(jc + 1, 1)
                                    sdone[jc + 1] = (nsA, nsB)
                                odone[jc] = (oA, oB)
                                c.mm(psX[:, 0:512], qt[:, jc * 128:
                                                       (jc + 1) * 128],
                                     ktA[:, ib * 1024:ib * 1024 + 512])
                            c.wait("VEC", s["pe"])
                            for it in range(2):
                                c.inc(c.tt(
                                    ot[:, p * TLOC + ib * 1024 + it * 512:
                                       p * TLOC + ib * 1024 + (it + 1) * 512],
                                    psO[:, it * 512:(it + 1) * 512],
                                    rhat[:, it * 512:(it + 1) * 512],
                                    ALU.mult), s["vec"])
                            c.wait("PE", s["vec"])

                run_phase(nc, blk, sems, p1 if "p1" in PHASES else (lambda c: None))

            with contextlib.ExitStack() as es2:
                ec = es2.enter_context
                wpsb = ec(nc.sbuf_tensor("wpsb", [128, CCH * C], BF16))
                xls = ec(nc.sbuf_tensor("xls", [128, 2 * CCH * 512], F32))
                x1t = ec(nc.sbuf_tensor("x1t", [128, CCH * 512], F32))
                x1bf = ec(nc.sbuf_tensor("x1bf", [128, CCH * 512], BF16))
                msb = ec(nc.sbuf_tensor("msb", [128, 512], F32))
                rsb = ec(nc.sbuf_tensor("rsb", [128, 512], F32))
                mrow = ec(nc.sbuf_tensor("mrow", [1, 512], F32))
                vrow = ec(nc.sbuf_tensor("vrow", [1, 512], F32))
                sq2 = ec(nc.sbuf_tensor("sq2", [128, 2 * 512], F32))
                smx = ec(nc.sbuf_tensor("smx", [128, 4], F32))
                prb = ec(nc.sbuf_tensor("prb", [128, 16], F32))
                cfm = ec(nc.sbuf_tensor("cfm", [8, TLOC], F32))
                psP = ec(nc.psum_tensor("psP", [128, 1024], F32))
                psM = ec(nc.psum_tensor("psM", [128, 512], F32))
                psR2 = ec(nc.psum_tensor("psR2", [128, 512], F32))
                psSt = ec(nc.psum_tensor("psSt", [128, 512], F32))
                psRt = ec(nc.psum_tensor("psRt", [128, 8], F32))
                blk = ec(nc.Block())
                def p2(c: Ctx):
                    s = c.s
                    if P2STAGE != -1:
                        c.barrier("pe", "act", "vec")
                    if P2STAGE != -2:
                        c.wait("GP", s["pe"])
                        wpd = wpsb[:].rearrange("p (c m) -> p c m", c=CCH)
                        wps = wproj[:].rearrange("(c p) m -> p c m", p=128)
                        c.inc(c.dma("GP", wpd[:, 0:4, :], wps[:, 0:4, :]),
                              s["dWp"], 16)
                        c.inc(c.dma("GP", wpd[:, 4:8, :], wps[:, 4:8, :]),
                              s["dWp"], 16)
                    evco = {}
                    for tt in range(min(CCH, P2TTS)):
                        c.barrier("pe", "act", "vec")
                        buf = tt % 2
                        xv = xls[:, buf * CCH * 512:(buf + 1) * CCH * 512]
                        c.wait("SY", s["vec"])
                        xsrc2 = (xT[:, tt * 512:(tt + 1) * 512]
                                 .rearrange("(c p) t -> p c t", p=128))
                        xdst2 = xv.rearrange("p (c t) -> p c t", c=CCH)
                        c.inc(c.dma("SY", xdst2[:, 0:4, :], xsrc2[:, 0:4, :]),
                              s["dA"], 16)
                        c.inc(c.dma("SY", xdst2[:, 4:8, :], xsrc2[:, 4:8, :]),
                              s["dA"], 16)
                        c.wait("VEC", s["dS"])
                        if tt == 0:
                            c.wait("PE", s["dWp"])
                        if P2STAGE < 1:
                            continue
                        for co in range(CCH):
                            pp = psP[:, (co % 2) * 512:(co % 2) * 512 + 512]
                            if co >= 2:
                                c.wait("PE", s["vec"], evco[co - 2])
                            for ci in range(CCH):
                                ins = c.mm(pp,
                                           wpsb[:, ci * C + co * 128:
                                                ci * C + (co + 1) * 128],
                                           ot[:, ci * TLOC + tt * 512:
                                              ci * TLOC + (tt + 1) * 512],
                                           start=(ci == 0), stop=(ci == CCH - 1))
                            c.inc(ins, s["pe"])
                            c.wait("VEC", s["pe"])
                            c.wait("VEC", s["dA"])
                            c.inc(c.stt(x1t[:, co * 512:(co + 1) * 512], pp,
                                        sbp[:, co:co + 1],
                                        xv[:, co * 512:(co + 1) * 512],
                                        ALU.add, ALU.add), s["vec"])
                            evco[co] = s["vec"].c
                        x1_done = s["vec"].c
                        if P2STAGE < 2:
                            continue
                        c.wait("PE", s["vec"], x1_done)
                        for ci in range(CCH):
                            ins = c.mm(psSt[0:1, 0:512], onescol[:],
                                       x1t[:, ci * 512:(ci + 1) * 512],
                                       start=(ci == 0), stop=(ci == CCH - 1))
                        c.inc(ins, s["pe"])
                        c.wait("VEC", s["pe"])
                        c.inc(c.ts(mrow[:], psSt[0:1, 0:512], 1.0 / C, None,
                                   ALU.mult), s["vec"])
                        c.wait("PE", s["vec"])
                        c.inc(c.mm(psM[:], ones1[:], mrow[:]), s["pe"])
                        c.wait("VEC", s["pe"])
                        c.inc(c.vcopy(msb[:], psM[:]), s["vec"])
                        c.wait("VEC", s["vec"])
                        for ci in range(CCH):
                            c.inc(c.tt(xv[:, ci * 512:(ci + 1) * 512],
                                       x1t[:, ci * 512:(ci + 1) * 512],
                                       msb[:], ALU.subtract), s["vec"])
                        d_done = s["vec"].c
                        if P2STAGE < 3:
                            continue
                        sqmm = {}
                        for ci in range(CCH):
                            c.wait("ACT", s["vec"], d_done)
                            if ci >= 2:
                                c.wait("ACT", s["pe"], sqmm[ci - 2])
                            c.inc(c.act(sq2[:, (ci % 2) * 512:
                                            (ci % 2) * 512 + 512],
                                        xv[:, ci * 512:(ci + 1) * 512],
                                        AF.Square), s["act"])
                            c.wait("PE", s["act"])
                            c.inc(c.mm(psSt[0:1, 0:512], onescol[:],
                                       sq2[:, (ci % 2) * 512:
                                           (ci % 2) * 512 + 512],
                                       start=(ci == 0), stop=(ci == CCH - 1)),
                                  s["pe"])
                            sqmm[ci] = s["pe"].c
                        c.wait("VEC", s["pe"])
                        c.inc(c.ts(vrow[:], psSt[0:1, 0:512], 1.0 / C, None,
                                   ALU.mult), s["vec"])
                        c.wait("ACT", s["vec"])
                        c.inc(c.act(vrow[:], vrow[:], AF.Ln, bias=kc[0:1, 0:1]),
                              s["act"])
                        c.wait("ACT", s["act"])
                        c.inc(c.act(vrow[:], vrow[:], AF.Exp,
                                    scale=kc[0:1, 1:2]), s["act"])
                        c.wait("PE", s["act"])
                        c.inc(c.mm(psR2[:], ones1[:], vrow[:]), s["pe"])
                        c.wait("VEC", s["pe"])
                        c.inc(c.vcopy(rsb[:], psR2[:]), s["vec"])
                        c.wait("VEC", s["vec"])
                        for ci in range(CCH):
                            c.inc(c.tt(x1t[:, ci * 512:(ci + 1) * 512],
                                       xv[:, ci * 512:(ci + 1) * 512],
                                       rsb[:], ALU.mult), s["vec"])
                        norm_done = s["vec"].c
                        c.wait("VEC", s["vec"], norm_done)
                        for ci in range(CCH):
                            c.inc(c.vcopy(x1bf[:, ci * 512:(ci + 1) * 512],
                                          x1t[:, ci * 512:(ci + 1) * 512]),
                                  s["vec"])
                        bf_done = s["vec"].c
                        if P2STAGE < 4:
                            continue
                        c.wait("SY", s["vec"], bf_done)
                        spfd = (spf[:, tt * 512:(tt + 1) * 512]
                                .rearrange("(c p) t -> p c t", p=128))
                        x1s = x1t[:].rearrange("p (c t) -> p c t", c=CCH)
                        spbd = (spb[:, tt * 512:(tt + 1) * 512]
                                .rearrange("(c p) t -> p c t", p=128))
                        x1bs = x1bf[:].rearrange("p (c t) -> p c t", c=CCH)
                        for lo, hi in ((0, 4), (4, 8)):
                            c.inc(c.dma("SY", spfd[:, lo:hi, :],
                                        x1s[:, lo:hi, :]), s["dS"], 16)
                            c.inc(c.dma("SY", spbd[:, lo:hi, :],
                                        x1bs[:, lo:hi, :]), s["dS"], 16)
                        if P2STAGE < 5:
                            continue
                        for t4 in range(4):
                            c.wait("PE", s["vec"], norm_done)
                            c.wait("PE", s["vec"])
                            c.wait("PE", s["act"])
                            for ci in range(CCH):
                                ins = c.mm(psRt[:],
                                           x1t[:, ci * 512 + t4 * 128:
                                               ci * 512 + (t4 + 1) * 128],
                                           swr[:, ci * E:(ci + 1) * E],
                                           start=(ci == 0), stop=(ci == CCH - 1))
                            c.inc(ins, s["pe"])

                            def v1(ins):
                                c.inc(ins, s["vec"])
                                c.wait("VEC", s["vec"])
                            c.wait("VEC", s["pe"])
                            v1(c.vred(smx[:, 0:1], psRt[:], ALU.max))
                            v1(c.ts(smx[:, 1:2], smx[:, 0:1], -1.0, None,
                                    ALU.mult))
                            c.wait("ACT", s["vec"])
                            c.wait("ACT", s["pe"])
                            c.inc(c.act(prb[:, 0:8], psRt[:], AF.Exp,
                                        bias=smx[:, 1:2]), s["act"])
                            c.wait("VEC", s["act"])
                            v1(c.vred(smx[:, 2:3], prb[:, 0:8], ALU.add))
                            v1(c.vrecip(smx[:, 3:4], smx[:, 2:3]))
                            v1(c.ts(prb[:, 0:8], prb[:, 0:8], smx[:, 3:4],
                                    None, ALU.mult))
                            v1(c.vred(smx[:, 0:1], prb[:, 0:8], ALU.max))
                            v1(c.ts(prb[:, 8:16], prb[:, 0:8], smx[:, 0:1],
                                    None, ALU.is_ge))
                            v1(c.stt(prb[:, 8:16], prb[:, 8:16], -1e30,
                                     prb[:, 0:8], ALU.mult, ALU.add))
                            v1(c.vred(smx[:, 1:2], prb[:, 8:16], ALU.max))
                            v1(c.ts(prb[:, 8:16], prb[:, 0:8], smx[:, 1:2],
                                    None, ALU.is_ge))
                            c.inc(c.tt(ctm[:, (tt * 4 + t4) * E:
                                            (tt * 4 + t4 + 1) * E],
                                       prb[:, 0:8], prb[:, 8:16], ALU.mult),
                                  s["vec"])
                    if P2STAGE < 6:
                        return
                    c.wait("PE", s["vec"])
                    for k in range(TLOC // 128):
                        if k >= 1:
                            c.wait("PE", s["vec"])
                        c.inc(c.tr(psSt[0:8, 0:128], ctm[:, k * E:(k + 1) * E],
                                   idf[:]), s["pe"])
                        c.wait("VEC", s["pe"])
                        c.inc(c.vcopy(cfm[:, k * 128:(k + 1) * 128],
                                      psSt[0:8, 0:128]), s["vec"])
                    c.wait("SY", s["vec"])
                    c.inc(c.dma("SY", cfmh[:], cfm[:]), s["dS"], 16)
                    marks["p2_stores"] = s["dS"].c

                run_phase(nc, blk, sems, p2 if "p2" in PHASES else (lambda c: None))

        for hs2 in range(2):
            with nc.sbuf_tensor(f"acc{hs2}", [128, CCH * 2048], F32) as acc:
                with contextlib.ExitStack() as es3:
                    ec = es3.enter_context
                    x1b = ec(nc.sbuf_tensor(f"x1b{hs2}", [128, CCH * 2048], BF16))
                    hbf = ec(nc.sbuf_tensor(f"hbf{hs2}", [128, HC2 * 512], BF16))
                    cfsl = ec(nc.sbuf_tensor(f"cfsl{hs2}", [1, 2 * 512], F32))
                    sb2e = ec(nc.sbuf_tensor(f"sb2e{hs2}", [1, C], F32))
                    w1s = ec(nc.sbuf_tensor(f"w1s{hs2}", [128, CCH * 2048], BF16))
                    w2s = ec(nc.sbuf_tensor(f"w2s{hs2}", [128, HC2 * C], BF16))
                    crp = ec(nc.sbuf_tensor(f"crp{hs2}", [128, 2048], F32))
                    psH = ec(nc.psum_tensor(f"psH{hs2}", [128, 2048], F32))
                    psY = ec(nc.psum_tensor(f"psY{hs2}", [128, 2048], F32))
                    blk = ec(nc.Block())
                    def p3a(c: Ctx, hs2=hs2, acc=acc, x1b=x1b, hbf=hbf,
                            cfsl=cfsl, sb2e=sb2e, w1s=w1s, w2s=w2s, crp=crp,
                            psH=psH, psY=psY):
                        s = c.s
                        c.barrier("pe", "act", "vec")
                        c.wait("SY", s["dS"], marks.get("p2_stores", 0))
                        accd = acc[:].rearrange("p (c t) -> p c t", c=CCH)
                        spfs = (spf[:, hs2 * 2048:(hs2 + 1) * 2048]
                                .rearrange("(c p) t -> p c t", p=128))
                        x1bd = x1b[:].rearrange("p (c t) -> p c t", c=CCH)
                        spbs = (spb[:, hs2 * 2048:(hs2 + 1) * 2048]
                                .rearrange("(c p) t -> p c t", p=128))
                        x1b_tgt = {}
                        acc_tgt = {}
                        for lsl in range(NSLAB):
                            c.inc(c.dma("SY",
                                        x1bd[:, :, lsl * 512:(lsl + 1) * 512],
                                        spbs[:, :, lsl * 512:(lsl + 1) * 512]),
                                  s["dA"], 16)
                            x1b_tgt[lsl] = s["dA"].c
                        for lsl in range(NSLAB):
                            c.inc(c.dma("SY",
                                        accd[:, :, lsl * 512:(lsl + 1) * 512],
                                        spfs[:, :, lsl * 512:(lsl + 1) * 512]),
                                  s["dA"], 16)
                            acc_tgt[lsl] = s["dA"].c
                        prevA = 0
                        neh = [0]
                        for e in range(E):
                            if neh[0] >= P3EHS:
                                break
                            c.wait("SY", s["pe"])
                            c.inc(c.dma("SY", sb2e[:], b2[e:e + 1, :]),
                                  s["dA"], 16)
                            for hh in range(2):
                                if neh[0] >= P3EHS:
                                    break
                                neh[0] += 1
                                if P3STAGE < 2:
                                    continue
                                c.wait("GP", s["pe"], prevA)
                                w1d = w1s[:].rearrange("p (c h) -> p c h",
                                                       c=CCH)
                                w1src = (w1[e][:, hh * 2048:(hh + 1) * 2048]
                                         .rearrange("(c p) h -> p c h", p=128))
                                for lo, hi in ((0, 4), (4, 8)):
                                    c.inc(c.dma("GP", w1d[:, lo:hi, :],
                                                w1src[:, lo:hi, :]),
                                          s["dW1"], 16)
                                c.wait("GP", s["pe"])
                                w2d = w2s[:].rearrange("p (h d) -> p h d",
                                                       h=HC2)
                                w2src = (w2[e][hh * 2048:(hh + 1) * 2048, :]
                                         .rearrange("(h p) d -> p h d", p=128))
                                for lo, hi in ((0, 4), (4, 8), (8, 12),
                                               (12, 16)):
                                    c.inc(c.dma("GP", w2d[:, lo:hi, :],
                                                w2src[:, lo:hi, :]),
                                          s["dW2"], 16)
                                dW1_tgt = s["dW1"].c
                                dW2_tgt = s["dW2"].c
                                cf_tgt = {}
                                c.wait("SY", s["pe"])
                                c.inc(c.dma("SY", cfsl[:, 0:512],
                                            cfmh[e:e + 1,
                                                 hs2 * 2048:hs2 * 2048 + 512]),
                                      s["dA"], 16)
                                cf_tgt[0] = s["dA"].c
                                if hh == 0:
                                    crp_ready = {}
                                    crp_all = {}
                                for sl in range(NSLAB):
                                    if P3STAGE < 3:
                                        continue
                                    cfv = cfsl[:, (sl % 2) * 512:
                                               (sl % 2) * 512 + 512]
                                    if hh == 1:
                                        crp_done = crp_all[sl]
                                        crpv = crp[:, sl * 512:
                                                   (sl + 1) * 512]
                                    if hh == 1:
                                        pass
                                    elif sl + 1 < NSLAB:
                                        c.wait("SY", s["pe"])
                                        c.inc(c.dma(
                                            "SY",
                                            cfsl[:, ((sl + 1) % 2) * 512:
                                                 ((sl + 1) % 2) * 512 + 512],
                                            cfmh[e:e + 1,
                                                 hs2 * 2048 + (sl + 1) * 512:
                                                 hs2 * 2048 + (sl + 2) * 512]),
                                            s["dA"], 16)
                                        cf_tgt[sl + 1] = s["dA"].c
                                    c.wait("PE", s["dA"], x1b_tgt[sl])
                                    if hh == 0:
                                        if sl == 0:
                                            c.wait("PE", s["dA"], cf_tgt[0])
                                            c.wait("PE", s["vec"])
                                            c.inc(c.mm(psH[:, 0:512], ones1[:],
                                                       cfv), s["pe"])
                                            c.wait("VEC", s["pe"])
                                            c.inc(c.vcopy(crp[:, 0:512],
                                                          psH[:, 0:512]),
                                                  s["vec"])
                                            crp_ready[0] = s["vec"].c
                                        crp_done = crp_ready[sl]
                                        crp_all[sl] = crp_done
                                        crpv = crp[:, sl * 512:
                                                   (sl + 1) * 512]
                                    if P3STAGE < 4:
                                        continue
                                    c.wait("PE", s["dW1"], dW1_tgt)
                                    evh = {}
                                    scl = s["vec"].c
                                    scl_q = {}
                                    for hc in range(HC2):
                                        if hc == 0:
                                            c.wait("PE", s["vec"], crp_done)
                                        if hc >= 4:
                                            c.wait("PE", s["vec"], evh[hc - 4])
                                        for ci in range(CCH):
                                            ins = c.mm(
                                                psH[:, (hc % 4) * 512:
                                                    (hc % 4) * 512 + 512],
                                                w1s[:, ci * 2048 + hc * 128:
                                                    ci * 2048 + (hc + 1) * 128],
                                                x1b[:, ci * 2048 + sl * 512:
                                                    ci * 2048 + (sl + 1) * 512],
                                                start=(ci == 0),
                                                stop=(ci == CCH - 1))
                                        c.inc(ins, s["pe"])
                                        c.wait("VEC", s["pe"])
                                        c.inc(c.ts(
                                            hbf[:, hc * 512:(hc + 1) * 512],
                                            psH[:, (hc % 4) * 512:
                                                (hc % 4) * 512 + 512],
                                            sb1[:, e * 32 + hh * 16 + hc:
                                                e * 32 + hh * 16 + hc + 1],
                                            None, ALU.add), s["vec"])
                                        evh[hc] = s["vec"].c
                                        if hc == 7 and sl + 1 < NSLAB \
                                                and hh == 0 and P3STAGE >= 3:
                                            c.wait("PE", s["dA"],
                                                   cf_tgt[sl + 1])
                                            c.inc(c.mm(
                                                psY[:, 1536:2048], ones1[:],
                                                cfsl[:, ((sl + 1) % 2) * 512:
                                                     ((sl + 1) % 2) * 512
                                                     + 512]), s["pe"])
                                            c.wait("VEC", s["pe"])
                                            c.inc(c.vcopy(
                                                crp[:, (sl + 1) * 512:
                                                    (sl + 2) * 512],
                                                psY[:, 1536:2048]), s["vec"])
                                            crp_ready[sl + 1] = s["vec"].c
                                        if P3STAGE >= 5 and hc % 4 == 3:
                                            q0 = (hc - 3) * 512
                                            if hc == 15:
                                                c.wait("ACT", s["vec"],
                                                       evh[13])
                                                c.inc(c.act(
                                                    hbf[:, q0:q0 + 1024],
                                                    hbf[:, q0:q0 + 1024],
                                                    AF.Gelu), s["act"])
                                                gq = s["act"].c
                                                c.wait("VEC", s["act"], gq)
                                                c.wait("VEC", s["vec"],
                                                       crp_done)
                                                for k4 in range(2):
                                                    qq = q0 + k4 * 512
                                                    c.inc(c.tt(
                                                        hbf[:, qq:qq + 512],
                                                        hbf[:, qq:qq + 512],
                                                        crpv, ALU.mult),
                                                        s["vec"])
                                                c.wait("ACT", s["vec"],
                                                       evh[15])
                                                c.inc(c.act(
                                                    hbf[:, q0 + 1024:
                                                        q0 + 2048],
                                                    hbf[:, q0 + 1024:
                                                        q0 + 2048],
                                                    AF.Gelu), s["act"])
                                                gq = s["act"].c
                                                c.wait("VEC", s["act"], gq)
                                                for k4 in range(2, 4):
                                                    qq = q0 + k4 * 512
                                                    c.inc(c.tt(
                                                        hbf[:, qq:qq + 512],
                                                        hbf[:, qq:qq + 512],
                                                        crpv, ALU.mult),
                                                        s["vec"])
                                                scl = s["vec"].c
                                                scl_q[hc // 4] = scl
                                            else:
                                                c.wait("ACT", s["vec"],
                                                       evh[hc])
                                                c.inc(c.act(
                                                    hbf[:, q0:q0 + 2048],
                                                    hbf[:, q0:q0 + 2048],
                                                    AF.Gelu), s["act"])
                                                gq = s["act"].c
                                                c.wait("VEC", s["act"], gq)
                                                c.wait("VEC", s["vec"],
                                                       crp_done)
                                                for k4 in range(4):
                                                    qq = q0 + k4 * 512
                                                    c.inc(c.tt(
                                                        hbf[:, qq:qq + 512],
                                                        hbf[:, qq:qq + 512],
                                                        crpv, ALU.mult),
                                                        s["vec"])
                                                scl = s["vec"].c
                                                scl_q[hc // 4] = scl
                                    if sl == NSLAB - 1:
                                        prevA = s["pe"].c
                                    if P3STAGE < 5:
                                        continue
                                    if P3STAGE < 6:
                                        continue
                                    c.wait("PE", s["dW2"], dW2_tgt)
                                    c.wait("PE", s["vec"], scl_q.get(0, scl))
                                    evy = {}
                                    for dc in range(CCH):
                                        yb = psY[:, (dc % 4) * 512:
                                                 (dc % 4) * 512 + 512]
                                        if dc >= 4:
                                            c.wait("PE", s["vec"], evy[dc - 4])
                                        first = True
                                        if hh == 0:
                                            ins = c.mm(
                                                yb,
                                                sb2e[:, dc * 128:(dc + 1) * 128],
                                                cfv, start=True, stop=False)
                                            first = False
                                        for hc in range(HC2):
                                            if dc == 0 and hc % 4 == 0 \
                                                    and hc > 0:
                                                c.wait("PE", s["vec"],
                                                       scl_q.get(hc // 4, scl))
                                            ins = c.mm(
                                                yb,
                                                w2s[:, hc * C + dc * 128:
                                                    hc * C + (dc + 1) * 128],
                                                hbf[:, hc * 512:(hc + 1) * 512],
                                                start=first and (hc == 0),
                                                stop=(hc == HC2 - 1))
                                            first = False
                                        c.inc(ins, s["pe"])
                                        c.wait("VEC", s["pe"])
                                        c.wait("VEC", s["dA"], acc_tgt[sl])
                                        c.inc(c.stt(
                                            acc[:, dc * 2048 + sl * 512:
                                                dc * 2048 + (sl + 1) * 512],
                                            yb, 1.0,
                                            acc[:, dc * 2048 + sl * 512:
                                                dc * 2048 + (sl + 1) * 512],
                                            ALU.mult, ALU.add), s["vec"])
                                        evy[dc] = s["vec"].c

                    run_phase(nc, blk, sems,
                              p3a if "p3a" in PHASES else (lambda c: None))

                with contextlib.ExitStack() as es4:
                    ec = es4.enter_context
                    ld2 = ec(nc.sbuf_tensor(f"ld2{hs2}", [128, CCH * 512], F32))
                    osg = ec(nc.sbuf_tensor(f"osg{hs2}", [128, 4 * C], F32))
                    msb2 = ec(nc.sbuf_tensor(f"msb2{hs2}", [128, 512], F32))
                    mrw = ec(nc.sbuf_tensor(f"mrw{hs2}", [1, 512], F32))
                    sq3 = ec(nc.sbuf_tensor(f"sq3{hs2}", [128, 2 * 512], F32))
                    psS2 = ec(nc.psum_tensor(f"psS2{hs2}", [128, 512], F32))
                    psM2 = ec(nc.psum_tensor(f"psM2{hs2}", [128, 512], F32))
                    psT2 = ec(nc.psum_tensor(f"psT2{hs2}", [128, 1024], F32))
                    blk = ec(nc.Block())
                    def p3b(c: Ctx, hs2=hs2, acc=acc, ld2=ld2, osg=osg,
                            msb2=msb2, mrw=mrw, sq3=sq3, psS2=psS2,
                            psM2=psM2, psT2=psT2):
                        s = c.s
                        c.barrier("pe", "act", "vec")
                        for sl in range(NSLAB):
                            def asl(ci):
                                return acc[:, ci * 2048 + sl * 512:
                                           ci * 2048 + (sl + 1) * 512]
                            if P3BSTAGE < 1:
                                continue
                            c.wait("PE", s["vec"])
                            for ci in range(CCH):
                                ins = c.mm(psS2[0:1, 0:512], onescol[:],
                                           asl(ci), start=(ci == 0),
                                           stop=(ci == CCH - 1))
                            c.inc(ins, s["pe"])
                            c.wait("VEC", s["pe"])
                            c.inc(c.ts(mrw[:], psS2[0:1, 0:512], 1.0 / C, None,
                                       ALU.mult), s["vec"])
                            c.wait("PE", s["vec"])
                            c.inc(c.mm(psM2[:], ones1[:], mrw[:]), s["pe"])
                            c.wait("VEC", s["pe"])
                            c.inc(c.vcopy(msb2[:], psM2[:]), s["vec"])
                            c.wait("VEC", s["vec"])
                            for ci in range(CCH):
                                c.inc(c.tt(ld2[:, ci * 512:(ci + 1) * 512],
                                           asl(ci), msb2[:], ALU.subtract),
                                      s["vec"])
                            d_done = s["vec"].c
                            if P3BSTAGE < 2:
                                continue
                            sqmm = {}
                            for ci in range(CCH):
                                c.wait("ACT", s["vec"], d_done)
                                if ci >= 2:
                                    c.wait("ACT", s["pe"], sqmm[ci - 2])
                                c.inc(c.act(sq3[:, (ci % 2) * 512:
                                                (ci % 2) * 512 + 512],
                                            ld2[:, ci * 512:(ci + 1) * 512],
                                            AF.Square), s["act"])
                                c.wait("PE", s["act"])
                                c.inc(c.mm(psS2[0:1, 0:512], onescol[:],
                                           sq3[:, (ci % 2) * 512:
                                               (ci % 2) * 512 + 512],
                                           start=(ci == 0),
                                           stop=(ci == CCH - 1)), s["pe"])
                                sqmm[ci] = s["pe"].c
                            c.wait("VEC", s["pe"])
                            c.inc(c.ts(mrw[:], psS2[0:1, 0:512], 1.0 / C, None,
                                       ALU.mult), s["vec"])
                            c.wait("ACT", s["vec"])
                            c.inc(c.act(mrw[:], mrw[:], AF.Ln,
                                        bias=kc[0:1, 0:1]), s["act"])
                            c.wait("ACT", s["act"])
                            c.inc(c.act(mrw[:], mrw[:], AF.Exp,
                                        scale=kc[0:1, 1:2]), s["act"])
                            c.wait("PE", s["act"])
                            c.inc(c.mm(psM2[:], ones1[:], mrw[:]), s["pe"])
                            c.wait("VEC", s["pe"])
                            c.inc(c.vcopy(msb2[:], psM2[:]), s["vec"])
                            c.wait("VEC", s["vec"])
                            if P3BSTAGE < 3:
                                continue
                            for ci in range(CCH):
                                c.inc(c.tt(ld2[:, ci * 512:(ci + 1) * 512],
                                           ld2[:, ci * 512:(ci + 1) * 512],
                                           msb2[:], ALU.mult), s["vec"])
                            nrm = s["vec"].c
                            c.wait("PE", s["vec"], nrm)
                            if P3BSTAGE < 4:
                                continue
                            c.wait("PE", s["dS"])
                            cpd = {}
                            for t4 in range(4):
                                for ci in range(CCH):
                                    idx = t4 * CCH + ci
                                    slot = (idx % 2) * 512
                                    if idx >= 2:
                                        c.wait("PE", s["vec"], cpd[idx - 2])
                                    c.inc(c.tr(psT2[:, slot:slot + 128],
                                               ld2[:, ci * 512 + t4 * 128:
                                                   ci * 512 + (t4 + 1) * 128],
                                               idf[:]), s["pe"])
                                    c.wait("VEC", s["pe"])
                                    c.inc(c.vcopy(
                                        osg[:, t4 * C + ci * 128:
                                            t4 * C + (ci + 1) * 128],
                                        psT2[:, slot:slot + 128]),
                                        s["vec"])
                                    cpd[idx] = s["vec"].c
                            if P3BSTAGE < 5:
                                continue
                            c.wait("SY", s["vec"])
                            c.inc(c.dma("SY",
                                        out[hs2 * 2048 + sl * 512:
                                            hs2 * 2048 + (sl + 1) * 512, :]
                                        .rearrange("(x p) c2 -> p x c2", p=128),
                                        osg[:].rearrange("p (x c2) -> p x c2",
                                                         x=4)),
                                  s["dS"], 16)
                        if hs2 == 1:
                            c.wait("SY", s["dS"])

                    run_phase(nc, blk, sems,
                              p3b if "p3b" in PHASES else (lambda c: None))

    return nc


_CACHED = {}


def kernel(L0_attn_x, L0_attn_Wqkv, L0_attn_bqkv, L0_attn_r_sigma,
           L0_attn_Wproj, L0_attn_bproj, L0_ffn_Wr, L0_ffn_W1, L0_ffn_b1,
           L0_ffn_W2, L0_ffn_b2):
    x = np.asarray(L0_attn_x, dtype=np.float32)
    wqkv = np.ascontiguousarray(np.asarray(L0_attn_Wqkv, dtype=np.float32))
    bqkv = np.asarray(L0_attn_bqkv, dtype=np.float32)
    sig = np.asarray(L0_attn_r_sigma, dtype=np.float32).reshape(1, 1)
    wproj = np.ascontiguousarray(np.asarray(L0_attn_Wproj, dtype=np.float32))
    bproj = np.asarray(L0_attn_bproj, dtype=np.float32)
    wr = np.asarray(L0_ffn_Wr, dtype=np.float32)
    w1 = np.ascontiguousarray(np.asarray(L0_ffn_W1, dtype=np.float32))
    b1 = np.asarray(L0_ffn_b1, dtype=np.float32)
    w2 = np.ascontiguousarray(np.asarray(L0_ffn_W2, dtype=np.float32))
    b2 = np.asarray(L0_ffn_b2, dtype=np.float32)

    bqkvT = np.ascontiguousarray(bqkv.reshape(24, 128).T)
    bprojT = np.ascontiguousarray(bproj.reshape(8, 128).T)
    wrT = np.ascontiguousarray(wr.T)
    b1T = np.ascontiguousarray(
        np.stack([b1[e].reshape(32, 128).T for e in range(E)], axis=1)
        .reshape(128, E * 32))

    if "nc" not in _CACHED:
        _CACHED["nc"] = build_program()
    nc = _CACHED["nc"]

    in_maps = []
    xTs = {}
    for core in range(NCORE):
        b = core // 2
        hs = core % 2
        key = (b, hs)
        if key not in xTs:
            xt = np.ascontiguousarray(x[b].T)
            if hs == 1:
                xt = np.ascontiguousarray(np.roll(xt, -4096, axis=1))
            xTs[key] = xt
        in_maps.append({
            "xT": xTs[key], "wqkv": wqkv, "bqkvT": bqkvT, "sigma": sig,
            "wproj": wproj, "bprojT": bprojT, "wrT": wrT, "w1": w1,
            "b1T": b1T, "w2": w2, "b2": b2,
        })

    res = run_bass_kernel_spmd(nc, in_maps, core_ids=list(range(NCORE)))
    outp = np.empty((B, T, C), dtype=np.float32)
    for core in range(NCORE):
        b = core // 2
        hs = core % 2
        outp[b, hs * 4096:(hs + 1) * 4096, :] = res.results[core]["out"]
    return outp
